# version 49
# baseline (speedup 1.0000x reference)
import os
import numpy as np
import ml_dtypes
import concourse.bass as bass
import concourse.mybir as mybir
from concourse.bass_utils import run_bass_kernel_spmd

F32 = mybir.dt.float32
BF16 = mybir.dt.bfloat16
AF = mybir.ActivationFunctionType
ALU = mybir.AluOpType
AX = mybir.AxisListType

T, D, HH = 4096, 1024, 8
C = 256
NCH = T // C
MT = C // 128
KF = D // 128
NQKVG = 6144
EPS = 1e-5

ENGS = ("sync", "gpsimd", "tensor", "vector", "scalar")
ESEM = {"sync": "sx", "gpsimd": "gp", "tensor": "pe", "vector": "dv", "scalar": "ac"}
DSEM = {"sync": "dx", "gpsimd": "dw", "scalar": "da"}
KLVL = int(os.environ.get("KLVL", "7"))
KNCH = int(os.environ.get("KNCH", str(NCH)))


def _iv(ap):
    t = ap.tensor.name
    off = ap.offset
    reach = 1
    for s, c in list(ap.ap):
        reach += (int(c) - 1) * abs(int(s))
    sz = mybir.dt.size(ap.dtype)
    return (t, int(off) * sz, (int(off) + reach) * sz)


class Sched:

    def __init__(self):
        self.ops = []
        self.recs = {}

    def __init_lanes(self):
        pass

    def op(self, eng, fn, reads=(), writes=(), dma=False):
        oid = len(self.ops)
        deps = set()
        ivr = [_iv(a) for a in reads]
        ivw = [_iv(a) for a in writes]
        if dma:
            if not hasattr(self, "lane_idx"):
                self.lane_idx = {}
            ln = self.lane_idx.get(eng, 0)
            self.lane_idx[eng] = (ln + 1) % self.NLANE
            ekey = (eng, ln)
        else:
            ln = None
            ekey = eng

        for (t, lo, hi) in ivr:
            m = self.recs.get(t)
            if m:
                for (kind, peng), rec in m.items():
                    if kind == "w" and rec[0] < hi and lo < rec[1]:
                        deps.add(rec[2])
        for (t, lo, hi) in ivw:
            m = self.recs.get(t)
            if m:
                for (kind, peng), rec in m.items():
                    if rec[0] < hi and lo < rec[1]:
                        deps.add(rec[2])
        for (t, lo, hi) in ivr:
            m = self.recs.setdefault(t, {})
            rec = m.get(("r", ekey))
            m[("r", ekey)] = [lo if rec is None else min(lo, rec[0]),
                              hi if rec is None else max(hi, rec[1]), oid]
        for (t, lo, hi) in ivw:
            m = self.recs.setdefault(t, {})
            rec = m.get(("w", ekey))
            m[("w", ekey)] = [lo if rec is None else min(lo, rec[0]),
                              hi if rec is None else max(hi, rec[1]), oid]
        deps = {d for d in deps
                if not (eng == "tensor" and self.ops[d]["eng"] == "tensor")}
        self.ops.append({"eng": eng, "fn": fn, "dma": dma, "deps": deps, "lane": ln})
        return oid

    def dma(self, eng, fn, reads=(), writes=()):
        return self.op(eng, fn, reads, writes, dma=True)

    NLANE = 8

    def emit(self, nc):
        needed = set()
        for o in self.ops:
            needed.update(o["deps"])
        val = {}
        cnt = {e: 0 for e in ENGS}
        lane_cnt = {}
        pre_wait = {}
        for i, o in enumerate(self.ops):
            e = o["eng"]
            if o["dma"]:
                sname = f"{DSEM[e]}{o['lane']}"
                u = lane_cnt.get(sname, 0)
                if u:
                    pre_wait[i] = (sname, 16 * u)
                lane_cnt[sname] = u + 1
                val[i] = (sname, 16 * (u + 1))
            elif i in needed:
                cnt[e] += 1
                val[i] = (ESEM[e], cnt[e])
        q = {e: [] for e in ENGS}
        watermark = {}
        for i, o in enumerate(self.ops):
            e = o["eng"]
            per = {}
            for d in o["deps"]:
                s, v = val[d]
                per[s] = max(per.get(s, 0), v)
            if i in pre_wait:
                s, v = pre_wait[i]
                per[s] = max(per.get(s, 0), v)
            for s in sorted(per):
                v = per[s]
                if watermark.get((e, s), 0) >= v:
                    continue
                watermark[(e, s)] = v
                q[e].append(("w", s, v))
            if i in val:
                q[e].append(("i", o["fn"], val[i][0], 16 if o["dma"] else 1))
            else:
                q[e].append(("o", o["fn"]))

        import contextlib
        with contextlib.ExitStack() as stack:
            SM = {}
            names = ["pe", "dv", "ac", "gp", "sx"]
            names += [f"dx{i}" for i in range(self.NLANE)]
            names += [f"dw{i}" for i in range(self.NLANE)]
            names += [f"da{i}" for i in range(self.NLANE)]
            for name in names:
                SM[name] = stack.enter_context(nc.semaphore(name))
            block = stack.enter_context(nc.Block())

            def run_queue(E, items, tail=()):
                for it in items:
                    if it[0] == "w":
                        E.wait_ge(SM[it[1]], it[2])
                    elif it[0] == "i":
                        it[1](E).then_inc(SM[it[2]], it[3])
                    else:
                        it[1](E)
                for s, v in tail:
                    if v:
                        E.wait_ge(SM[s], v)

            tail = [(s, v * 16) for s, v in
                    ((k, lane_cnt.get(k, 0)) for k in names if k[0] == "d") if v]

            @block.sync
            def _(E):
                run_queue(E, q["sync"], tail)

            @block.gpsimd
            def _(E):
                run_queue(E, q["gpsimd"])

            @block.tensor
            def _(E):
                run_queue(E, q["tensor"])

            @block.vector
            def _(E):
                run_queue(E, q["vector"])

            @block.scalar
            def _(E):
                run_queue(E, q["scalar"])
        return nc


def build_nc():
    nc = bass.Bass("TRN2", target_bir_lowering=False)

    d_xt = nc.dram_tensor("xt", [128, NCH, KF, C], F32, kind="ExternalInput").ap()
    d_wqkvg = nc.dram_tensor("wqkvg", [D, NQKVG], F32, kind="ExternalInput").ap()
    d_wproj = nc.dram_tensor("wproj", [2048, D], F32, kind="ExternalInput").ap()
    d_tabs = nc.dram_tensor("tabs", [128, NCH, MT, 2048], BF16, kind="ExternalInput").ap()
    d_ident = nc.dram_tensor("ident_d", [128, 128], BF16, kind="ExternalInput").ap()
    d_tri = nc.dram_tensor("tri_d", [128, 128], BF16, kind="ExternalInput").ap()
    d_iota = nc.dram_tensor("iota_d", [1, C], F32, kind="ExternalInput").ap()
    d_decay = nc.dram_tensor("decay_d", [1, 8], F32, kind="ExternalInput").ap()
    d_lng = nc.dram_tensor("lng_d", [128, 8], F32, kind="ExternalInput").ap()
    d_lnb = nc.dram_tensor("lnb_d", [128, 8], F32, kind="ExternalInput").ap()
    d_bnm = nc.dram_tensor("bnm_d", [128, 8], F32, kind="ExternalInput").ap()
    d_bnv = nc.dram_tensor("bnv_d", [128, 8], F32, kind="ExternalInput").ap()
    d_bng = nc.dram_tensor("bng_d", [128, 8], F32, kind="ExternalInput").ap()
    d_bnb = nc.dram_tensor("bnb_d", [128, 8], F32, kind="ExternalInput").ap()
    d_wr = nc.dram_tensor("wr", [128, 64], F32, kind="ExternalInput").ap()
    d_br = nc.dram_tensor("br_d", [1, 8], F32, kind="ExternalInput").ap()
    d_wd = nc.dram_tensor("wd_d", [128, 4096], F32, kind="ExternalInput").ap()
    d_bd = nc.dram_tensor("bd_d", [8, 64], F32, kind="ExternalInput").ap()
    d_wu = nc.dram_tensor("wu_d", [64, 8192], F32, kind="ExternalInput").ap()
    d_bu = nc.dram_tensor("bu_d", [128, 64], F32, kind="ExternalInput").ap()

    d_wqkvg_bf = nc.dram_tensor("wqkvg_bf", [128, 12, KF, 512], BF16, kind="Internal").ap()
    d_wproj_bf = nc.dram_tensor("wproj_bf", [2048, D], BF16, kind="Internal").ap()
    d_xp = nc.dram_tensor("xp", [128, NCH, KF, C], F32, kind="Internal").ap()
    d_zt = nc.dram_tensor("zt", [128, NCH, KF, C], BF16, kind="Internal").ap()
    d_out = nc.dram_tensor("outT", [128, NCH, KF, C], F32, kind="ExternalOutput").ap()

    A = nc.alloc_sbuf_tensor
    x_sb = [A(f"x{i}", [128, KF, C], F32) for i in range(3)]
    tab_sb = [A(f"tab{i}", [128, MT, 2048], BF16) for i in range(2)]
    wnb_sb = [A(f"wnb{i}", [128, 4, 512], BF16)[:] for i in range(8)]
    wp_sb = [A(f"wp{i}", [128, 2, D], BF16) for i in range(4)]
    tmp_f = A("tmp_f", [128, KF, C], F32)
    lnx_sb = A("lnx_sb", [128, KF, C], BF16)
    rows = A("rows", [1, 1792], F32)
    qbf = [A(f"qbf{i}", [128, 1024], BF16) for i in range(2)]
    rt1 = A("rt1", [128, 512], BF16)
    rt2 = A("rt2", [128, 512], BF16)
    ck_sb = A("ck_sb", [128, 512], BF16)
    sk_sb = A("sk_sb", [128, 512], BF16)
    qs_sb = A("qs_sb", [128, MT, 1024], BF16)
    ks_sb = A("ks_sb", [128, MT, 1024], BF16)
    xbf_fl = qs_sb[:].rearrange("p m n -> p (m n)")
    xbf_sb = xbf_fl.rearrange("p (k t) -> p k t", t=C)
    qT_sb = A("qT_sb", [128, HH * C], BF16)
    kT_sb = A("kT_sb", [128, HH * C], BF16)
    v_sb = A("v_sb", [128, MT, 2048], BF16)
    gate_sb = A("gate_sb", [128, MT, 2048], BF16)
    sT_sb = A("sT8", [128, HH, 384], BF16)
    ao_sb = A("ao_sb", [128, MT, 2048], BF16)
    xsq_fl = ao_sb[:].rearrange("p m n -> p (m n)")[:, 0:KF * C]
    xsq_sb = xsq_fl.rearrange("p (k t) -> p k t", t=C)
    gst = A("gst", [128, 96], F32)
    gg_sb = A("gg_sb", [128, MT, 2048], BF16)
    rows_b = gg_sb[:].rearrange("p m n -> p (m n)")[0:1, 0:3584].bitcast(F32)
    ggT_sb = A("ggT_sb", [128, 16, C], BF16)
    st_bf = A("st_bf", [128, HH * 256], BF16)
    st_tmp = A("st_tmp", [128, 256], F32)
    xp_sb = A("xp_sb0", [128, KF, C], F32)
    bc_sb = A("bc_sb", [128, 512], F32)
    pe_sc = [A(f"pe_sc{i}", [128, 256], F32) for i in range(2)]
    out_sb = A("out_sb", [128, KF, C], F32)
    zbuf = [tmp_f[:].rearrange("p k t -> p (k t)").bitcast(BF16)[
        :, i * KF * C:(i + 1) * KF * C].rearrange("p (k t) -> p k t", t=C)
        for i in range(2)]
    sc_sb = A("sc_sb", [128, 256], F32)
    ident = A("ident", [128, 128], BF16)
    tri = A("tri", [128, 128], BF16)
    iota = A("iota", [1, C], F32)
    decay = A("decay", [1, 8], F32)
    ldrow = A("ldrow", [1, 8], F32)
    dcrow = A("dcrow", [1, 8], F32)
    dc_bc = A("dc_bc", [128, 8], F32)
    dnegj = A("dnegj", [128, MT, 8], F32)
    d2i = A("d2i", [128, MT, 8], F32)
    d1i = A("d1i", [128, MT, 8], F32)
    ones_bf = A("ones_bf", [128, 1], BF16)
    eps_c = A("eps_c", [128, 1], F32)
    ones_f = A("ones_f", [1, 128], F32)
    ones_c = A("ones_c", [128, 1], F32)
    lng = A("lng", [128, 8], F32)
    lnb = A("lnb", [128, 8], F32)
    bnm = A("bnm", [128, 8], F32)
    bnv = A("bnv", [128, 8], F32)
    bng = A("bng", [128, 8], F32)
    bnb = A("bnb", [128, 8], F32)
    wr_sb = A("wr_sb", [128, 64], F32)
    br_sb = A("br_sb", [1, 8], F32)
    bd_sb = A("bd_sb", [8, 64], F32)
    bu_sb = A("bu_sb", [128, 64], F32)
    we_sb = [wnb_sb[i].rearrange("p k n -> p (k n)").bitcast(F32)[:, 0:512] for i in range(2)]
    wue_sb = [tab_sb[i][:].rearrange("p m q -> p (m q)").bitcast(F32)[0:64, 0:1024] for i in range(2)]
    acc = [qbf[0][:].bitcast(F32), qbf[1][:].bitcast(F32)]
    uacc = [gate_sb[0:64, 0, :].bitcast(F32), gate_sb[0:64, 1, :].bitcast(F32)]
    wd_bf = qs_sb[:, 0, 0:512]
    wu_bf = ks_sb[0:64, 0, :]
    ysent = A("ysent", [128, 8], F32)
    zs_sb = A("zs_sb", [128, 8], F32)
    zb_sb = A("zb_sb", [128, 8], F32)
    sm_sb = A("sm_sb", [1, 32], F32)
    ohT = A("ohT", [8, 1], F32)
    ohbc = A("ohbc", [128, 8], F32)
    gatebc = A("gatebc", [128, 1], F32)
    biash = A("biash", [64, 1], F32)
    bug = [A("bug_a", [128, 8], F32), A("bug_b", [128, 8], F32)]
    h_bf2 = [A("h_bf0", [64, 256], BF16)[:],
             tab_sb[0][:].rearrange("p m q -> p (m q)")[0:64, 0:256]]

    PA = nc.alloc_psum_tensor("PA", [128, 1024], F32)
    PB = nc.alloc_psum_tensor("PB", [128, 1024], F32)
    PC = nc.alloc_psum_tensor("PC", [128, 512], F32)
    PD = nc.alloc_psum_tensor("PD", [128, 512], F32)
    PE_ = nc.alloc_psum_tensor("PE_", [128, 512], F32)
    PT = nc.alloc_psum_tensor("PT", [128, 1024], BF16)

    S = Sched()

    tg_ctr = [0]

    def trans_group(srcs, dst_view, drain="vector"):
        n = len(srcs)
        assert n <= 8
        PTx = PT[:] if tg_ctr[0] % 2 == 0 else PD[:].bitcast(BF16)
        tg_ctr[0] += 1
        for j, sap in enumerate(srcs):
            S.op("tensor", lambda E, s=sap, j=j, PTx=PTx: E.transpose(
                PTx[:, j * 128:(j + 1) * 128], s, ident[:]),
                 reads=[sap, ident[:]], writes=[PTx[:, j * 128:(j + 1) * 128]])
        pv = PTx.rearrange("p (j n) -> p j n", n=128)[:, 0:n, :]
        if drain == "vector":
            S.op("vector", lambda E, d=dst_view, pv=pv: E.tensor_copy(d, pv),
                 reads=[pv], writes=[dst_view])
        else:
            S.op("scalar", lambda E, d=dst_view, pv=pv: E.activation(d, pv, AF.Copy),
                 reads=[pv], writes=[dst_view])

    for t_, dsrc in [(ident, d_ident), (tri, d_tri), (iota, d_iota), (decay, d_decay),
                     (lng, d_lng), (lnb, d_lnb), (bnm, d_bnm), (bnv, d_bnv),
                     (bng, d_bng), (bnb, d_bnb), (wr_sb, d_wr), (br_sb, d_br),
                     (bd_sb, d_bd), (bu_sb, d_bu)]:
        S.dma("sync", lambda E, a=t_, b=dsrc: E.dma_start(out=a[:], in_=b),
              reads=[dsrc], writes=[t_[:]])

    for nb in range(12):
        iv_ = d_wqkvg.rearrange("(k p) n -> p k n", p=128)[:, :, nb * 512:(nb + 1) * 512]
        ov_ = d_wqkvg_bf[:, nb]
        S.dma("gpsimd", lambda E, iv_=iv_, ov_=ov_: E.dma_start(out=ov_, in_=iv_),
              reads=[iv_], writes=[ov_])
    for i in range(16):
        S.dma("gpsimd", lambda E, i=i: E.dma_start(
            out=d_wproj_bf[i * 128:(i + 1) * 128, :],
            in_=d_wproj[i * 128:(i + 1) * 128, :]),
            reads=[d_wproj[i * 128:(i + 1) * 128, :]],
            writes=[d_wproj_bf[i * 128:(i + 1) * 128, :]])

    for t_, v_ in [(ones_bf, 1.0), (eps_c, EPS), (ones_f, 1.0), (ones_c, 1.0),
                   (st_bf, 0.0), (ysent, 0.0)]:
        S.op("vector", lambda E, t=t_, v=v_: E.memset(t[:], v), writes=[t_[:]])

    S.op("scalar", lambda E: E.activation(ldrow[:], decay[:], AF.Ln),
         reads=[decay[:]], writes=[ldrow[:]])
    S.op("scalar", lambda E: E.activation(dcrow[:], ldrow[:], AF.Exp, scale=float(C)),
         reads=[ldrow[:]], writes=[dcrow[:]])
    for m in range(MT):
        S.op("tensor", lambda E, m=m: E.matmul(
            PC[:, 0:8], iota[:, m * 128:(m + 1) * 128], ldrow[:], start=True, stop=True),
            reads=[iota[:, m * 128:(m + 1) * 128], ldrow[:]], writes=[PC[:, 0:8]])
        for dst, sc in [(dnegj, -1.0), (d1i, 1.0), (d2i, 2.0)]:
            S.op("scalar", lambda E, m=m, d=dst, sc=sc: E.activation(
                d[:, m, :], PC[:, 0:8], AF.Exp, scale=sc),
                reads=[PC[:, 0:8]], writes=[dst[:, m, :]])
    S.op("tensor", lambda E: E.matmul(PC[:, 8:16], ones_f[:], dcrow[:], start=True, stop=True),
         reads=[ones_f[:], dcrow[:]], writes=[PC[:, 8:16]])
    S.op("vector", lambda E: E.tensor_copy(dc_bc[:], PC[:, 8:16]),
         reads=[PC[:, 8:16]], writes=[dc_bc[:]])

    def ln_cast_half(xbuf, half):
        a, b = half * (KF // 2) * C, (half + 1) * (KF // 2) * C
        xv = xbuf[:, half * (KF // 2):(half + 1) * (KF // 2), :]
        xf = xbuf[:].rearrange("p k t -> p (k t)")[:, a:b]
        S.op("scalar", lambda E, xf=xf, a=a, b=b: E.activation(
            xbf_fl[:, a:b], xf, AF.Copy),
            reads=[xv], writes=[xbf_fl[:, a:b]])
        S.op("vector", lambda E, a=a, b=b: E.tensor_tensor(
            xsq_fl[:, a:b], xbf_fl[:, a:b], xbf_fl[:, a:b], ALU.mult),
            reads=[xbf_fl[:, a:b]], writes=[xsq_fl[:, a:b]])

    def ln_cast(xbuf):
        ln_cast_half(xbuf, 0)
        ln_cast_half(xbuf, 1)

    def ln_stats_mms(k0, k1):
        for src, off in ((xbf_sb, 0), (xsq_sb, C)):
            for k in range(k0, k1):
                S.op("tensor", lambda E, k=k, s=src, off=off: E.matmul(
                    PC[0:1, off:off + C], ones_bf[:], s[:, k, :],
                    start=(k == 0), stop=(k == KF - 1)),
                    reads=[ones_bf[:], src[:, k, :]], writes=[PC[0:1, off:off + C]])

    def ln_stats_rows(rows, combined):
        S.op("scalar", lambda E: E.activation(rows[:, 512:1024], PC[0:1, 0:512], AF.Copy,
                                              scale=1.0 / D),
             reads=[PC[0:1, 0:512]], writes=[rows[:, 512:1024]])
        S.op("vector", lambda E: E.tensor_tensor(rows[:, 1024:1280], rows[:, 512:768],
                                                 rows[:, 512:768], ALU.mult),
             reads=[rows[:, 512:768]], writes=[rows[:, 1024:1280]])
        S.op("vector", lambda E: E.tensor_sub(rows[:, 1280:1536], rows[:, 768:1024],
                                              rows[:, 1024:1280]),
             reads=[rows[:, 768:1280]], writes=[rows[:, 1280:1536]])
        S.op("scalar", lambda E: E.activation(rows[:, 1536:1792], rows[:, 1280:1536],
                                              AF.Ln, bias=eps_c[0:1, 0:1]),
             reads=[rows[:, 1280:1536], eps_c[0:1, 0:1]], writes=[rows[:, 1536:1792]])
        S.op("scalar", lambda E: E.activation(rows[:, 0:256], rows[:, 1536:1792],
                                              AF.Exp, scale=-0.5),
             reads=[rows[:, 1536:1792]], writes=[rows[:, 0:256]])
        if combined:
            S.op("scalar", lambda E: E.activation(rows[:, 1024:1280], rows[:, 1536:1792],
                                                  AF.Exp, scale=-1.0),
                 reads=[rows[:, 1536:1792]], writes=[rows[:, 1024:1280]])
            S.op("vector", lambda E: E.tensor_tensor(rows[:, 1024:1280], rows[:, 1280:1536],
                                                     rows[:, 1024:1280], ALU.mult),
                 reads=[rows[:, 1280:1536], rows[:, 1024:1280]], writes=[rows[:, 1024:1280]])
            S.op("scalar", lambda E: E.activation(rows[:, 1024:1280], rows[:, 1024:1280],
                                                  AF.Ln, bias=eps_c[0:1, 0:1]),
                 reads=[rows[:, 1024:1280], eps_c[0:1, 0:1]], writes=[rows[:, 1024:1280]])
            S.op("vector", lambda E: E.tensor_add(rows[:, 1024:1280], rows[:, 1024:1280],
                                                  rows[:, 1536:1792]),
                 reads=[rows[:, 1024:1280], rows[:, 1536:1792]], writes=[rows[:, 1024:1280]])
            S.op("scalar", lambda E: E.activation(rows[:, 0:256], rows[:, 1024:1280],
                                                  AF.Exp, scale=-0.5),
                 reads=[rows[:, 1024:1280]], writes=[rows[:, 0:256]])
        S.op("vector", lambda E: E.tensor_tensor(rows[:, 256:512], rows[:, 512:768],
                                                 rows[:, 0:256], ALU.mult),
             reads=[rows[:, 512:768], rows[:, 0:256]], writes=[rows[:, 256:512]])
    def ln_apply(xbuf, dst, rows, Pv=None):
        if Pv is None:
            Pv = PA[:, 0:512]
        S.op("tensor", lambda E, Pv=Pv: E.matmul(Pv, ones_f[:], rows[:, 0:512],
                                                 start=True, stop=True),
             reads=[ones_f[:], rows[:, 0:512]], writes=[Pv])
        rbc = Pv[:, 0:256].unsqueeze(1).broadcast_to([128, KF, C])
        mbc = Pv[:, 256:512].unsqueeze(1).broadcast_to([128, KF, C])
        S.op("vector", lambda E, xb=xbuf, rbc=rbc: E.tensor_tensor(
            tmp_f[:], xb[:], rbc, ALU.mult),
            reads=[xbuf[:], Pv[:, 0:256]], writes=[tmp_f[:]])
        S.op("vector", lambda E, dst=dst, mbc=mbc: E.tensor_tensor(
            dst[:], tmp_f[:], mbc, ALU.subtract),
            reads=[tmp_f[:], Pv[:, 256:512]], writes=[dst[:]])

    def load_x(c, dest, src_dram):
        xv = src_dram[:, c]
        S.dma("scalar", lambda E, o=dest, i=xv: E.dma_start(out=o[:], in_=i),
              reads=[xv], writes=[dest[:]])

    def load_tab(c):
        tv = d_tabs[:, c]
        S.dma("scalar", lambda E, o=tab_sb[c % 2], i=tv: E.dma_start(out=o[:], in_=i),
              reads=[tv], writes=[tab_sb[c % 2][:]])

    wnb_ctr = [0]

    def load_wnb_half(nb, half, q="sync"):
        buf = wnb_sb[wnb_ctr[0] % 8]
        wnb_ctr[0] += 1
        wv = d_wqkvg_bf[:, nb, half * 4:(half + 1) * 4, :]
        S.dma(q, lambda E, o=buf, i=wv: E.dma_start(out=o, in_=i),
              reads=[wv], writes=[buf])
        return buf

    def load_wp(vtp):
        iv_ = d_wproj_bf[vtp * 256:(vtp + 1) * 256, :].rearrange(
            "(two p) n -> p two n", p=128)
        S.dma("sync", lambda E, o=wp_sb[vtp % 4], i=iv_: E.dma_start(out=o[:], in_=i),
              reads=[d_wproj_bf[vtp * 256:(vtp + 1) * 256, :]], writes=[wp_sb[vtp % 4][:]])

    qk_bufs = {}
    preloaded = set()

    def preload(c):
        if c in preloaded or c >= NCH:
            return
        preloaded.add(c)
        load_x(c, x_sb[c % 3], d_xt)
        load_tab(c)

    def load_pair_bufs(nb0, q2="sync"):
        return {(0, 0): load_wnb_half(nb0, 0), (1, 0): load_wnb_half(nb0 + 1, 0, q2),
                (0, 1): load_wnb_half(nb0, 1), (1, 1): load_wnb_half(nb0 + 1, 1, q2)}

    def qk_pre(c):
        for nb0 in (0, 2):
            qk_bufs[(c, nb0)] = load_pair_bufs(nb0)

    def ln_stats(rows, combined):
        ln_stats_mms(0, KF)
        ln_stats_rows(rows, combined)

    def front_a(c):
        ln_cast(x_sb[c % 3])
        ln_stats(rows, combined=False)

    def front_b(c):
        ln_apply(x_sb[c % 3], lnx_sb, rows, Pv=PT[:].bitcast(F32))

    def qk_phase(c):
        tb = tab_sb[c % 2]

        def qk_mms(nb0):
            psums = [PA, PB]
            if (c, nb0) in qk_bufs:
                bufs = qk_bufs.pop((c, nb0))
            else:
                bufs = load_pair_bufs(nb0)
            for k in range(KF):
                for m in range(MT):
                    P = psums[m]
                    for nbo in range(2):
                        wb = bufs[(nbo, k // 4)]
                        S.op("tensor", lambda E, m=m, nbo=nbo, k=k, P=P, wb=wb: E.matmul(
                            P[:, nbo * 512:(nbo + 1) * 512],
                            lnx_sb[:, k, m * 128:(m + 1) * 128],
                            wb[:, k % 4, :],
                            start=(k == 0), stop=(k == KF - 1)),
                            reads=[lnx_sb[:, k, m * 128:(m + 1) * 128], wb[:, k % 4, :]],
                            writes=[P[:, nbo * 512:(nbo + 1) * 512]])

        def rotary(psums, dst, fold_k):
            for m in range(MT):
                P, q = psums[m], qbf[m]
                S.op("scalar", lambda E, P=P, q=q: E.activation(q[:], P[:], AF.Copy),
                     reads=[P[:]], writes=[q[:]])
            for m in range(MT):
                q = qbf[m]
                qv = q[:].rearrange("p (h two r) -> p two h r", two=2, r=64)
                qe, qo = qv[:, 0], qv[:, 1]
                if fold_k:
                    dneg_v = dnegj[:, m, :].unsqueeze(2).broadcast_to([128, 8, 64])
                    for dstt, toff in ((ck_sb, 1024), (sk_sb, 1536)):
                        tv = tb[:, m, toff:toff + 512].rearrange("p (h r) -> p h r", r=64)
                        S.op("vector", lambda E, d=dstt, tv=tv, dv_=dneg_v: E.tensor_tensor(
                            d[:].rearrange("p (h r) -> p h r", r=64), tv, dv_, ALU.mult),
                            reads=[tb[:, m, toff:toff + 512], dnegj[:, m, :]],
                            writes=[dstt[:]])
                    cs = ck_sb[:].rearrange("p (h r) -> p h r", r=64)
                    ss = sk_sb[:].rearrange("p (h r) -> p h r", r=64)
                    cs_r, ss_r = ck_sb[:], sk_sb[:]
                else:
                    cs = tb[:, m, 0:512].rearrange("p (h r) -> p h r", r=64)
                    ss = tb[:, m, 512:1024].rearrange("p (h r) -> p h r", r=64)
                    cs_r, ss_r = tb[:, m, 0:512], tb[:, m, 512:1024]
                t1v = rt1[:].rearrange("p (h r) -> p h r", r=64)
                t2v = rt2[:].rearrange("p (h r) -> p h r", r=64)
                dvw = dst[:, m, :].rearrange("p (h two r) -> p two h r", two=2, r=64)
                de, do = dvw[:, 0], dvw[:, 1]
                S.op("vector", lambda E, a=qe, b=cs, t1v=t1v: E.tensor_tensor(
                    t1v, a, b, ALU.mult), reads=[q[:], cs_r], writes=[rt1[:]])
                S.op("vector", lambda E, a=qo, b=ss, t2v=t2v: E.tensor_tensor(
                    t2v, a, b, ALU.mult), reads=[q[:], ss_r], writes=[rt2[:]])
                S.op("vector", lambda E, o=de, t1v=t1v, t2v=t2v: E.tensor_tensor(
                    o, t1v, t2v, ALU.subtract),
                    reads=[rt1[:], rt2[:]], writes=[dst[:, m, :]])
                S.op("vector", lambda E, a=qo, b=cs, t1v=t1v: E.tensor_tensor(
                    t1v, a, b, ALU.mult), reads=[q[:], cs_r], writes=[rt1[:]])
                S.op("vector", lambda E, a=qe, b=ss, t2v=t2v: E.tensor_tensor(
                    t2v, a, b, ALU.mult), reads=[q[:], ss_r], writes=[rt2[:]])
                S.op("vector", lambda E, o=do, t1v=t1v, t2v=t2v: E.tensor_tensor(
                    o, t1v, t2v, ALU.add),
                    reads=[rt1[:], rt2[:]], writes=[dst[:, m, :]])

        qk_mms(0)
        rotary([PA, PB], qs_sb, False)
        qk_mms(2)
        rotary([PA, PB], ks_sb, True)

    def vgate_phase(c):
        def vg_pair(nb0, dst, act):
            slots = {(0, 0): PC[:], (0, 1): PD[:],
                     (1, 0): PA[:, 0:512], (1, 1): PA[:, 512:1024]}
            bufs = load_pair_bufs(nb0, q2="gpsimd")
            for k in range(KF):
                for m in range(MT):
                    for nbo in range(2):
                        P = slots[(m, nbo)]
                        wb = bufs[(nbo, k // 4)]
                        S.op("tensor", lambda E, m=m, k=k, P=P, wb=wb: E.matmul(
                            P, lnx_sb[:, k, m * 128:(m + 1) * 128], wb[:, k % 4, :],
                            start=(k == 0), stop=(k == KF - 1)),
                            reads=[lnx_sb[:, k, m * 128:(m + 1) * 128], wb[:, k % 4, :]],
                            writes=[P])
            for m in range(MT):
                for nbo in range(2):
                    nb = nb0 + nbo
                    off = (nb - (4 if dst is v_sb else 8)) * 512
                    P = slots[(m, nbo)]
                    S.op("scalar", lambda E, m=m, off=off, P=P, dst=dst, act=act: E.activation(
                        dst[:, m, off:off + 512], P, act),
                        reads=[P], writes=[dst[:, m, off:off + 512]])
        vg_pair(4, v_sb, AF.Copy)
        vg_pair(6, v_sb, AF.Copy)
        vg_pair(8, gate_sb, AF.Silu)
        vg_pair(10, gate_sb, AF.Silu)

    def qkt_phase(c):
        for m in range(MT):
            for src_, dstT in ((qs_sb, qT_sb), (ks_sb, kT_sb)):
                srcs = [src_[:, m, h * 128:(h + 1) * 128] for h in range(HH)]
                dv_ = dstT[:].rearrange("p (h n) -> p h n", n=128)[:, m::MT, :]
                trans_group(srcs, dv_, drain="scalar")

    def retention_phase(c):
        def scores(h):
            PS = PE_ if h % 2 == 0 else PA
            S.op("tensor", lambda E, h=h, PS=PS: E.matmul(
                PS[:, 0:256], kT_sb[:, h * C:h * C + 128], qT_sb[:, h * C:h * C + 256],
                start=True, stop=True),
                reads=[kT_sb[:, h * C:h * C + 128], qT_sb[:, h * C:h * C + 256]],
                writes=[PS[:, 0:256]])
            S.op("tensor", lambda E, h=h, PS=PS: E.matmul(
                PS[:, 256:384], kT_sb[:, h * C + 128:h * C + 256],
                qT_sb[:, h * C + 128:h * C + 256], start=True, stop=True),
                reads=[kT_sb[:, h * C + 128:h * C + 256], qT_sb[:, h * C + 128:h * C + 256]],
                writes=[PS[:, 256:384]])

        def sevac(h):
            PS = PE_ if h % 2 == 0 else PA
            sT = sT_sb[:, h, :]
            S.op("vector", lambda E, sT=sT, PS=PS: E.tensor_tensor(
                sT[:, 0:128], PS[:, 0:128], tri[:], ALU.mult),
                reads=[PS[:, 0:128], tri[:]], writes=[sT[:, 0:128]])
            S.op("vector", lambda E, sT=sT, PS=PS: E.tensor_tensor(
                sT[:, 256:384], PS[:, 256:384], tri[:], ALU.mult),
                reads=[PS[:, 256:384], tri[:]], writes=[sT[:, 256:384]])
            S.op("scalar", lambda E, sT=sT, PS=PS: E.activation(
                sT[:, 128:256], PS[:, 128:256], AF.Copy),
                reads=[PS[:, 128:256]], writes=[sT[:, 128:256]])

        def inner(h):
            PO = PD if h % 2 == 0 else PB
            sT = sT_sb[:, h, :]
            S.op("tensor", lambda E, h=h, sT=sT, PO=PO: E.matmul(
                PO[:, 0:256], sT[:, 0:128], v_sb[:, 0, h * 256:(h + 1) * 256],
                start=True, stop=False),
                reads=[sT[:, 0:128], v_sb[:, 0, h * 256:(h + 1) * 256]],
                writes=[PO[:, 0:256]])
            S.op("tensor", lambda E, h=h, PO=PO: E.matmul(
                PO[:, 0:256], qT_sb[:, h * C:h * C + 128], st_bf[:, h * 256:(h + 1) * 256],
                start=False, stop=True),
                reads=[qT_sb[:, h * C:h * C + 128], st_bf[:, h * 256:(h + 1) * 256]],
                writes=[PO[:, 0:256]])
            S.op("tensor", lambda E, h=h, sT=sT, PO=PO: E.matmul(
                PO[:, 256:512], sT[:, 128:256], v_sb[:, 0, h * 256:(h + 1) * 256],
                start=True, stop=False),
                reads=[sT[:, 128:256], v_sb[:, 0, h * 256:(h + 1) * 256]],
                writes=[PO[:, 256:512]])
            S.op("tensor", lambda E, h=h, sT=sT, PO=PO: E.matmul(
                PO[:, 256:512], sT[:, 256:384], v_sb[:, 1, h * 256:(h + 1) * 256],
                start=False, stop=False),
                reads=[sT[:, 256:384], v_sb[:, 1, h * 256:(h + 1) * 256]],
                writes=[PO[:, 256:512]])
            S.op("tensor", lambda E, h=h, PO=PO: E.matmul(
                PO[:, 256:512], qT_sb[:, h * C + 128:h * C + 256],
                st_bf[:, h * 256:(h + 1) * 256], start=False, stop=True),
                reads=[qT_sb[:, h * C + 128:h * C + 256], st_bf[:, h * 256:(h + 1) * 256]],
                writes=[PO[:, 256:512]])
            for i in range(MT):
                S.op("scalar", lambda E, h=h, i=i, PO=PO: E.activation(
                    ao_sb[:, i, h * 256:(h + 1) * 256], PO[:, i * 256:(i + 1) * 256], AF.Copy),
                    reads=[PO[:, i * 256:(i + 1) * 256]],
                    writes=[ao_sb[:, i, h * 256:(h + 1) * 256]])

        def kvstate(h):
            P = PC if h % 4 < 2 else PE_
            ko = (h % 2) * 256
            S.op("tensor", lambda E, h=h, ko=ko, P=P: E.matmul(
                P[:, ko:ko + 256], ks_sb[:, 0, h * 128:(h + 1) * 128],
                v_sb[:, 0, h * 256:(h + 1) * 256], start=True, stop=False),
                reads=[ks_sb[:, 0, h * 128:(h + 1) * 128], v_sb[:, 0, h * 256:(h + 1) * 256]],
                writes=[P[:, ko:ko + 256]])
            S.op("tensor", lambda E, h=h, ko=ko, P=P: E.matmul(
                P[:, ko:ko + 256], ks_sb[:, 1, h * 128:(h + 1) * 128],
                v_sb[:, 1, h * 256:(h + 1) * 256], start=False, stop=True),
                reads=[ks_sb[:, 1, h * 128:(h + 1) * 128], v_sb[:, 1, h * 256:(h + 1) * 256]],
                writes=[P[:, ko:ko + 256]])
            S.op("vector", lambda E, h=h, ko=ko, P=P: E.tensor_tensor(
                st_tmp[:], P[:, ko:ko + 256],
                st_bf[:, h * 256:(h + 1) * 256], ALU.add),
                reads=[P[:, ko:ko + 256], st_bf[:, h * 256:(h + 1) * 256]],
                writes=[st_tmp[:]])
            S.op("vector", lambda E, h=h: E.tensor_scalar(
                st_bf[:, h * 256:(h + 1) * 256], st_tmp[:], dc_bc[:, h:h + 1], None, ALU.mult),
                reads=[st_tmp[:], dc_bc[:, h:h + 1]],
                writes=[st_bf[:, h * 256:(h + 1) * 256]])

        for h in range(HH):
            scores(h)
            sevac(h)
        for h in range(HH):
            inner(h)
        for h in range(HH):
            kvstate(h)

    def gn_phase(c):
        aof = ao_sb[:].rearrange("p i n -> p (i n)")
        S.op("vector", lambda E, aof=aof: E.tensor_tensor(
            gg_sb[:].rearrange("p i n -> p (i n)"), aof, aof, ALU.mult),
            reads=[ao_sb[:]], writes=[gg_sb[:]])
        S.op("vector", lambda E: E.tensor_reduce(
            gst[:, 0:16], ao_sb[:].rearrange("p i (h n) -> p (i h) n", n=256),
            AX.X, ALU.add), reads=[ao_sb[:]], writes=[gst[:, 0:16]])
        S.op("vector", lambda E: E.tensor_reduce(
            gst[:, 16:32], gg_sb[:].rearrange("p i (h n) -> p (i h) n", n=256),
            AX.X, ALU.add), reads=[gg_sb[:]], writes=[gst[:, 16:32]])
        S.op("vector", lambda E: E.tensor_scalar(gst[:, 32:48], gst[:, 0:16],
                                                 -1.0 / 256, None, ALU.mult),
             reads=[gst[:, 0:16]], writes=[gst[:, 32:48]])
        S.op("vector", lambda E: E.tensor_scalar(gst[:, 64:80], gst[:, 16:32],
                                                 1.0 / 256, None, ALU.mult),
             reads=[gst[:, 16:32]], writes=[gst[:, 64:80]])
        S.op("vector", lambda E: E.tensor_tensor(gst[:, 80:96], gst[:, 32:48],
                                                 gst[:, 32:48], ALU.mult),
             reads=[gst[:, 32:48]], writes=[gst[:, 80:96]])
        S.op("vector", lambda E: E.tensor_sub(sc_sb[:, 0:16], gst[:, 64:80], gst[:, 80:96]),
             reads=[gst[:, 64:96]], writes=[sc_sb[:, 0:16]])
        S.op("vector", lambda E: E.tensor_tensor(
            gst[:, 64:80], sc_sb[:, 0:16], d2i[:].rearrange("p m h -> p (m h)"), ALU.mult),
            reads=[sc_sb[:, 0:16], d2i[:]], writes=[gst[:, 64:80]])
        S.op("scalar", lambda E: E.activation(gst[:, 80:96], gst[:, 64:80], AF.Ln,
                                              bias=eps_c[:]),
             reads=[gst[:, 64:80], eps_c[:]], writes=[gst[:, 80:96]])
        S.op("scalar", lambda E: E.activation(sc_sb[:, 16:32], gst[:, 80:96],
                                              AF.Exp, scale=-0.5),
             reads=[gst[:, 80:96]], writes=[sc_sb[:, 16:32]])
        S.op("vector", lambda E: E.tensor_tensor(
            gst[:, 48:64], sc_sb[:, 16:32], d1i[:].rearrange("p m h -> p (m h)"), ALU.mult),
            reads=[sc_sb[:, 16:32], d1i[:]], writes=[gst[:, 48:64]])
        for i in range(MT):
            for h in range(HH):
                col = i * 8 + h
                S.op("vector", lambda E, i=i, h=h, col=col: E.tensor_scalar(
                    sc_sb[:, 0:256], gate_sb[:, i, h * 256:(h + 1) * 256],
                    gst[:, 48 + col:49 + col], None, ALU.mult),
                    reads=[gate_sb[:, i, h * 256:(h + 1) * 256], gst[:, 48 + col:49 + col]],
                    writes=[sc_sb[:, 0:256]])
                S.op("vector", lambda E, i=i, h=h, col=col: E.scalar_tensor_tensor(
                    gg_sb[:, i, h * 256:(h + 1) * 256],
                    ao_sb[:, i, h * 256:(h + 1) * 256],
                    gst[:, 32 + col:33 + col], sc_sb[:, 0:256],
                    ALU.add, ALU.mult),
                    reads=[ao_sb[:, i, h * 256:(h + 1) * 256], gst[:, 32 + col:33 + col],
                           sc_sb[:, 0:256]],
                    writes=[gg_sb[:, i, h * 256:(h + 1) * 256]])

    def ggt_proj_phase(c, post_wave=None, mid_w1=None):
        for i in range(MT):
            for vg in range(2):
                srcs = [gg_sb[:, i, (vg * 8 + v) * 128:(vg * 8 + v + 1) * 128]
                        for v in range(8)]
                dv_ = ggT_sb[:, vg * 8:(vg + 1) * 8, i * 128:(i + 1) * 128]
                trans_group(srcs, dv_)
        xb = x_sb[c % 3]
        for wave in range(2):
            for vtp in range(8):
                if wave == 1 and vtp == 2 and mid_w1 is not None:
                    mid_w1()
                load_wp(vtp)
                for sub in range(2):
                    vt = 2 * vtp + sub
                    for fo_ in range(4):
                        f = wave * 4 + fo_
                        P = [PA, PB][fo_ // 2]
                        fo = (fo_ % 2) * 512
                        S.op("tensor", lambda E, vtp=vtp, sub=sub, vt=vt, f=f, P=P, fo=fo: E.matmul(
                            P[:, fo:fo + C], wp_sb[vtp % 4][:, sub, f * 128:(f + 1) * 128],
                            ggT_sb[:, vt, :], start=(vt == 0), stop=(vt == 15)),
                            reads=[wp_sb[vtp % 4][:, sub, f * 128:(f + 1) * 128],
                                   ggT_sb[:, vt, :]],
                            writes=[P[:, fo:fo + C]])
            for fo_ in range(4):
                f = wave * 4 + fo_
                P = [PA, PB][fo_ // 2]
                fo = (fo_ % 2) * 512
                sc = pe_sc[f % 2]
                S.op("scalar", lambda E, P=P, fo=fo, sc=sc: E.activation(
                    sc[:], P[:, fo:fo + C], AF.Copy),
                    reads=[P[:, fo:fo + C]], writes=[sc[:]])
                S.op("gpsimd", lambda E, f=f, xb=xb, sc=sc: E.tensor_tensor(
                    xp_sb[:, f, :], sc[:], xb[:, f, :], ALU.add),
                    reads=[sc[:], xb[:, f, :]], writes=[xp_sb[:, f, :]])
            if post_wave is not None:
                post_wave(wave)
        xpv = d_xp[:, c]
        S.dma("scalar", lambda E, o=xpv: E.dma_start(out=o, in_=xp_sb[:]),
              reads=[xp_sb[:]], writes=[xpv])

    def b1_phase_a(c):
        ln_stats(rows_b, combined=True)

    def b1_bcast(c):
        S.op("tensor", lambda E: E.matmul(PC[:], ones_f[:], rows_b[:, 0:512],
                                          start=True, stop=True),
             reads=[ones_f[:], rows_b[:, 0:512]], writes=[PC[:]])
        S.op("vector", lambda E: E.tensor_copy(bc_sb[:], PC[:]),
             reads=[PC[:]], writes=[bc_sb[:]])

    def b1_rest(c):
        rbc = bc_sb[:, 0:256].unsqueeze(1).broadcast_to([128, KF, C])
        mbc = bc_sb[:, 256:512].unsqueeze(1).broadcast_to([128, KF, C])
        S.op("vector", lambda E, rbc=rbc: E.tensor_tensor(
            tmp_f[:], xp_sb[:], rbc, ALU.mult),
            reads=[xp_sb[:], bc_sb[:, 0:256]], writes=[tmp_f[:]])
        S.op("vector", lambda E, mbc=mbc: E.tensor_tensor(
            out_sb[:], tmp_f[:], mbc, ALU.subtract),
            reads=[tmp_f[:], bc_sb[:, 256:512]], writes=[out_sb[:]])
        for k in range(KF):
            S.op("vector", lambda E, k=k: E.tensor_scalar(
                out_sb[:, k, :], out_sb[:, k, :], lng[:, k:k + 1], None, ALU.mult),
                reads=[out_sb[:, k, :], lng[:, k:k + 1]], writes=[out_sb[:, k, :]])
        S.op("vector", lambda E: E.tensor_reduce(gst[:, 0:8], out_sb[:], AX.X, ALU.add),
             reads=[out_sb[:]], writes=[gst[:, 0:8]])
        S.op("vector", lambda E: E.tensor_add(ysent[:], ysent[:], gst[:, 0:8]),
             reads=[ysent[:], gst[:, 0:8]], writes=[ysent[:]])
        zv = d_zt[:, c]
        S.dma("gpsimd", lambda E, o=zv: E.dma_start(out=o, in_=out_sb[:]),
              reads=[out_sb[:]], writes=[zv])

    if KLVL >= 1:
        NRUN = min(KNCH, NCH)
        preload(0)
        preload(1)
        front_a(0)
        front_b(0)
        qk_pre(0)
        qk_phase(0)
        vgate_phase(0)
        qkt_phase(0)
        if NRUN > 1:
            front_a(1)
            front_b(1)
        for c in range(NRUN):
            if c + 1 < NRUN:
                qk_pre(c + 1)
            preload(c + 2)
            retention_phase(c)
            if c + 1 < NRUN:
                qk_phase(c + 1)
            gn_phase(c)
            if c + 1 < NRUN:
                vgate_phase(c + 1)
                qkt_phase(c + 1)
            if KLVL >= 5 and c > 0:
                b1_rest(c - 1)
            if c + 2 < NRUN:
                front_a(c + 2)
            ggt_proj_phase(
                c, post_wave=(lambda w: ln_cast_half(xp_sb, w)) if KLVL >= 5 else None)
            if KLVL >= 5:
                b1_phase_a(c)
            if c + 2 < NRUN:
                front_b(c + 2)
            if KLVL >= 5:
                b1_bcast(c)
        if KLVL >= 5:
            b1_rest(NRUN - 1)

    if KLVL >= 7:
        for cc in range(2):
            zvp = d_zt[:, cc]
            S.dma("sync", lambda E, o=zbuf[cc], i=zvp: E.dma_start(out=o, in_=i),
                  reads=[zvp], writes=[zbuf[cc]])
            xvp = d_xp[:, cc]
            S.dma("gpsimd", lambda E, o=x_sb[cc], i=xvp: E.dma_start(
                out=o[:], in_=i), reads=[xvp], writes=[x_sb[cc][:]])
    if KLVL >= 6:
        S.op("vector", lambda E: E.tensor_scalar(zs_sb[:], ysent[:], 1.0 / T, None, ALU.mult),
             reads=[ysent[:]], writes=[zs_sb[:]])
        S.op("vector", lambda E: E.tensor_add(zs_sb[:], zs_sb[:], lnb[:]),
             reads=[zs_sb[:], lnb[:]], writes=[zs_sb[:]])
        S.op("vector", lambda E: E.tensor_sub(zb_sb[:], zs_sb[:], bnm[:]),
             reads=[zs_sb[:], bnm[:]], writes=[zb_sb[:]])
        S.op("scalar", lambda E: E.activation(gst[:, 0:8], bnv[:], AF.Ln, bias=eps_c[:]),
             reads=[bnv[:], eps_c[:]], writes=[gst[:, 0:8]])
        S.op("scalar", lambda E: E.activation(gst[:, 8:16], gst[:, 0:8], AF.Exp, scale=-0.5),
             reads=[gst[:, 0:8]], writes=[gst[:, 8:16]])
        S.op("vector", lambda E: E.tensor_tensor(zb_sb[:], zb_sb[:], gst[:, 8:16], ALU.mult),
             reads=[zb_sb[:], gst[:, 8:16]], writes=[zb_sb[:]])
        S.op("vector", lambda E: E.tensor_tensor(zb_sb[:], zb_sb[:], bng[:], ALU.mult),
             reads=[zb_sb[:], bng[:]], writes=[zb_sb[:]])
        S.op("vector", lambda E: E.tensor_add(zb_sb[:], zb_sb[:], bnb[:]),
             reads=[zb_sb[:], bnb[:]], writes=[zb_sb[:]])
        for k in range(KF):
            S.op("tensor", lambda E, k=k: E.matmul(
                PC[0:1, 0:8], zb_sb[:, k:k + 1], wr_sb[:, k * 8:(k + 1) * 8],
                start=(k == 0), stop=(k == KF - 1)),
                reads=[zb_sb[:, k:k + 1], wr_sb[:, k * 8:(k + 1) * 8]],
                writes=[PC[0:1, 0:8]])
        S.op("vector", lambda E: E.tensor_tensor(sm_sb[:, 0:8], PC[0:1, 0:8], br_sb[:], ALU.add),
             reads=[PC[0:1, 0:8], br_sb[:]], writes=[sm_sb[:, 0:8]])
        S.op("vector", lambda E: E.reduce_max(sm_sb[:, 24:25], sm_sb[:, 0:8], AX.X),
             reads=[sm_sb[:, 0:8]], writes=[sm_sb[:, 24:25]])
        S.op("vector", lambda E: E.tensor_scalar(sm_sb[:, 25:26], sm_sb[:, 24:25],
                                                 -1.0, None, ALU.mult),
             reads=[sm_sb[:, 24:25]], writes=[sm_sb[:, 25:26]])
        S.op("scalar", lambda E: E.activation(sm_sb[:, 8:16], sm_sb[:, 0:8], AF.Exp,
                                              bias=sm_sb[:, 25:26]),
             reads=[sm_sb[:, 0:8], sm_sb[:, 25:26]], writes=[sm_sb[:, 8:16]])
        S.op("vector", lambda E: E.reduce_sum(sm_sb[:, 26:27], sm_sb[:, 8:16], AX.X),
             reads=[sm_sb[:, 8:16]], writes=[sm_sb[:, 26:27]])
        S.op("vector", lambda E: E.reciprocal(sm_sb[:, 27:28], sm_sb[:, 26:27]),
             reads=[sm_sb[:, 26:27]], writes=[sm_sb[:, 27:28]])
        S.op("vector", lambda E: E.tensor_scalar(sm_sb[:, 16:24], sm_sb[:, 8:16],
                                                 1.0, None, ALU.is_ge),
             reads=[sm_sb[:, 8:16]], writes=[sm_sb[:, 16:24]])
        S.op("tensor", lambda E: E.transpose(PD[0:8, 0:1], sm_sb[:, 16:24], ones_f[0:1, 0:1]),
             reads=[sm_sb[:, 16:24], ones_f[0:1, 0:1]], writes=[PD[0:8, 0:1]])
        S.op("tensor", lambda E: E.matmul(PC[:, 0:8], ones_f[:], sm_sb[:, 16:24],
                                          start=True, stop=True),
             reads=[ones_f[:], sm_sb[:, 16:24]], writes=[PC[:, 0:8]])
        S.op("tensor", lambda E: E.matmul(PC[:, 8:9], ones_f[:], sm_sb[:, 27:28],
                                          start=True, stop=True),
             reads=[ones_f[:], sm_sb[:, 27:28]], writes=[PC[:, 8:9]])
        S.op("vector", lambda E: E.tensor_copy(ohT[:], PD[0:8, 0:1]),
             reads=[PD[0:8, 0:1]], writes=[ohT[:]])
        S.op("vector", lambda E: E.tensor_copy(ohbc[:], PC[:, 0:8]),
             reads=[PC[:, 0:8]], writes=[ohbc[:]])
        S.op("vector", lambda E: E.tensor_copy(gatebc[:], PC[:, 8:9]),
             reads=[PC[:, 8:9]], writes=[gatebc[:]])
        for e in range(8):
            S.dma("gpsimd", lambda E, e=e, o=we_sb[e % 2]: E.dma_start(
                out=o, in_=d_wd[:, e * 512:(e + 1) * 512]),
                reads=[d_wd[:, e * 512:(e + 1) * 512]], writes=[we_sb[e % 2]])
            S.dma("gpsimd", lambda E, e=e, o=wue_sb[e % 2]: E.dma_start(
                out=o, in_=d_wu[:, e * 1024:(e + 1) * 1024]),
                reads=[d_wu[:, e * 1024:(e + 1) * 1024]], writes=[wue_sb[e % 2]])
            if e == 0:
                S.op("vector", lambda E: E.tensor_scalar(
                    acc[0], we_sb[0], ohbc[:, 0:1], None, ALU.mult),
                    reads=[we_sb[0], ohbc[:, 0:1]], writes=[acc[0]])
                S.op("vector", lambda E: E.tensor_scalar(
                    uacc[0], wue_sb[0], ohbc[0:64, 0:1], None, ALU.mult),
                    reads=[wue_sb[0], ohbc[0:64, 0:1]], writes=[uacc[0]])
                S.op("vector", lambda E: E.tensor_scalar(
                    bug[0][:], bu_sb[:, 0:8], ohbc[:, 0:1], None, ALU.mult),
                    reads=[bu_sb[:, 0:8], ohbc[:, 0:1]], writes=[bug[0][:]])
            else:
                S.op("vector", lambda E, e=e: E.scalar_tensor_tensor(
                    acc[e % 2], we_sb[e % 2], ohbc[:, e:e + 1], acc[(e - 1) % 2],
                    ALU.mult, ALU.add),
                    reads=[we_sb[e % 2], ohbc[:, e:e + 1], acc[(e - 1) % 2]],
                    writes=[acc[e % 2]])
                S.op("vector", lambda E, e=e: E.scalar_tensor_tensor(
                    uacc[e % 2], wue_sb[e % 2], ohbc[0:64, e:e + 1], uacc[(e - 1) % 2],
                    ALU.mult, ALU.add),
                    reads=[wue_sb[e % 2], ohbc[0:64, e:e + 1], uacc[(e - 1) % 2]],
                    writes=[uacc[e % 2]])
                S.op("vector", lambda E, e=e: E.scalar_tensor_tensor(
                    bug[e % 2][:], bu_sb[:, e * 8:(e + 1) * 8], ohbc[:, e:e + 1],
                    bug[(e - 1) % 2][:], ALU.mult, ALU.add),
                    reads=[bu_sb[:, e * 8:(e + 1) * 8], ohbc[:, e:e + 1], bug[(e - 1) % 2][:]],
                    writes=[bug[e % 2][:]])
        wd_eff, wu_eff, bu_eff = acc[1], uacc[1], bug[1]
        S.op("vector", lambda E: E.tensor_copy(wd_bf, wd_eff),
             reads=[wd_eff], writes=[wd_bf])
        S.op("vector", lambda E: E.tensor_scalar(uacc[0], wu_eff, gatebc[0:64, 0:1],
                                                 None, ALU.mult),
             reads=[wu_eff, gatebc[0:64, 0:1]], writes=[uacc[0]])
        S.op("vector", lambda E: E.tensor_copy(wu_bf, uacc[0]),
             reads=[uacc[0]], writes=[wu_bf])
        S.op("vector", lambda E: E.tensor_scalar(bug[0][:], bu_eff[:], gatebc[:, 0:1],
                                                 None, ALU.mult),
             reads=[bu_eff[:], gatebc[:, 0:1]], writes=[bug[0][:]])
        bu_g = bug[0]
        S.op("vector", lambda E: E.tensor_scalar(sc_sb[0:8, 0:64], bd_sb[:], ohT[:],
                                                 None, ALU.mult),
             reads=[bd_sb[:], ohT[:]], writes=[sc_sb[0:8, 0:64]])
        for k in range(KF):
            S.op("tensor", lambda E, k=k: E.matmul(
                PC[0:1, 64:128], lnb[:, k:k + 1], wd_eff[:, k * 64:(k + 1) * 64],
                start=(k == 0), stop=False),
                reads=[lnb[:, k:k + 1], wd_eff[:, k * 64:(k + 1) * 64]],
                writes=[PC[0:1, 64:128]])
        S.op("tensor", lambda E: E.matmul(
            PC[0:1, 64:128], ones_c[0:8, :], sc_sb[0:8, 0:64], start=False, stop=True),
            reads=[ones_c[0:8, :], sc_sb[0:8, 0:64]], writes=[PC[0:1, 64:128]])
        S.op("vector", lambda E: E.tensor_copy(sc_sb[0:1, 0:64], PC[0:1, 64:128]),
             reads=[PC[0:1, 64:128]], writes=[sc_sb[0:1, 0:64]])
        S.op("tensor", lambda E: E.transpose(PD[0:64, 0:1], sc_sb[0:1, 0:64],
                                             ones_f[0:1, 0:1]),
             reads=[sc_sb[0:1, 0:64], ones_f[0:1, 0:1]], writes=[PD[0:64, 0:1]])
        S.op("vector", lambda E: E.tensor_copy(biash[:], PD[0:64, 0:1]),
             reads=[PD[0:64, 0:1]], writes=[biash[:]])

    if KLVL >= 7:
        def b2_h(c):
            zb = zbuf[c % 2]
            hb = h_bf2[c % 2]
            for k in range(KF):
                S.op("tensor", lambda E, k=k, zb=zb: E.matmul(
                    PC[0:64, 0:256], wd_bf[:, k * 64:(k + 1) * 64], zb[:, k, :],
                    start=(k == 0), stop=(k == KF - 1)),
                    reads=[wd_bf[:, k * 64:(k + 1) * 64], zb[:, k, :]],
                    writes=[PC[0:64, 0:256]])
            S.op("scalar", lambda E, hb=hb: E.activation(hb[:], PC[0:64, 0:256], AF.Relu,
                                                         bias=biash[:]),
                 reads=[PC[0:64, 0:256], biash[:]], writes=[hb[:]])
            if c + 2 < NCH:
                zv2 = d_zt[:, c + 2]
                S.dma("sync", lambda E, o=zbuf[(c + 2) % 2], i=zv2: E.dma_start(
                    out=o, in_=i), reads=[zv2], writes=[zbuf[(c + 2) % 2]])

        def b2_y(c):
            xb = x_sb[c % 3]
            hb = h_bf2[c % 2]
            for f in range(KF):
                P = PA if f < 4 else PB
                fo = (f % 4) * 256
                S.op("tensor", lambda E, f=f, P=P, fo=fo, hb=hb: E.matmul(
                    P[:, fo:fo + C], wu_bf[:, f * 128:(f + 1) * 128], hb[:],
                    start=True, stop=True),
                    reads=[wu_bf[:, f * 128:(f + 1) * 128], hb[:]],
                    writes=[P[:, fo:fo + C]])
            for f in range(KF):
                P = PA if f < 4 else PB
                fo = (f % 4) * 256
                S.op("vector", lambda E, f=f, P=P, fo=fo, xb=xb, bu_g=bu_g: E.scalar_tensor_tensor(
                    out_sb[:, f, :], P[:, fo:fo + C], bu_g[:, f:f + 1], xb[:, f, :],
                    ALU.add, ALU.add),
                    reads=[P[:, fo:fo + C], bu_g[:, f:f + 1], xb[:, f, :]],
                    writes=[out_sb[:, f, :]])
            ov = d_out[:, c]
            S.dma("scalar", lambda E, o=ov: E.dma_start(out=o, in_=out_sb[:]),
                  reads=[out_sb[:]], writes=[ov])
            if c + 2 < NCH:
                xv2 = d_xp[:, c + 2]
                S.dma("gpsimd", lambda E, o=x_sb[(c + 2) % 3], i=xv2: E.dma_start(
                    out=o[:], in_=i), reads=[xv2], writes=[x_sb[(c + 2) % 3][:]])

        for c in range(NCH):
            b2_h(c)
            if c > 0:
                b2_y(c - 1)
        b2_y(NCH - 1)

    return S.emit(nc)



def _host_constants():
    bf = ml_dtypes.bfloat16
    QK_DIM, SCALE_BASE = 1024, 512.0
    dim_half = QK_DIM // 2
    base_scale = ((np.arange(0, QK_DIM, 2) + 0.4 * QK_DIM) / (1.4 * QK_DIM)).astype(np.float64)
    pos = np.arange(T, dtype=np.float64)
    scale = (base_scale[None, :] ** (pos[:, None] / SCALE_BASE))
    inv_freq = 1.0 / 10000.0 ** (np.arange(dim_half) / dim_half)
    sinus = pos[:, None] * inv_freq[None, :]
    sin, cos = np.sin(sinus), np.cos(sinus)
    dkinv = (QK_DIM // HH) ** -0.5
    tabs = np.concatenate([cos * scale * dkinv, sin * scale * dkinv,
                           cos / scale, sin / scale], axis=1).astype(np.float32).astype(bf)
    ident = np.eye(128, dtype=np.float32).astype(bf)
    tri = np.triu(np.ones((128, 128), np.float32)).astype(bf)
    iota = np.arange(C, dtype=np.float32).reshape(1, C)
    return tabs, ident, tri, iota


_QKPERM = None


def _qk_perm():
    global _QKPERM
    if _QKPERM is None:
        perm = []
        for h in range(HH):
            perm += [128 * h + 2 * r for r in range(64)]
            perm += [128 * h + 2 * r + 1 for r in range(64)]
        _QKPERM = np.asarray(perm)
    return _QKPERM


def _col8(v):
    return np.ascontiguousarray(np.asarray(v, np.float32).reshape(8, 128).T)


def kernel(L0_attn_hidden_states, L0_attn_W_qkv, L0_attn_W_g, L0_attn_W_proj,
           L0_attn_decay, L0_ffn_ln_g, L0_ffn_ln_b, L0_ffn_bn_g, L0_ffn_bn_b,
           L0_ffn_bn_mean, L0_ffn_bn_var, L0_ffn_Wr, L0_ffn_br, L0_ffn_W_down,
           L0_ffn_b_down, L0_ffn_W_up, L0_ffn_b_up):
    x = np.asarray(L0_attn_hidden_states, np.float32)
    Wqkv = np.asarray(L0_attn_W_qkv, np.float32)
    Wg = np.asarray(L0_attn_W_g, np.float32)
    Wproj = np.ascontiguousarray(np.asarray(L0_attn_W_proj, np.float32))
    perm = _qk_perm()
    Wq = Wqkv[:, 0:1024][:, perm]
    Wk = Wqkv[:, 1024:2048][:, perm]
    Wv = Wqkv[:, 2048:4096]
    wqkvg = np.ascontiguousarray(np.concatenate([Wq, Wk, Wv, Wg], axis=1))

    tabs, ident, tri, iota = _host_constants()
    tabs = np.ascontiguousarray(
        tabs.reshape(NCH, MT, 128, 2048).transpose(2, 0, 1, 3))
    decay = np.asarray(L0_attn_decay, np.float32).reshape(1, 8)
    lng = _col8(L0_ffn_ln_g)
    lnb = _col8(L0_ffn_ln_b)
    bnm = _col8(L0_ffn_bn_mean)
    bnv = _col8(L0_ffn_bn_var)
    bng = _col8(L0_ffn_bn_g)
    bnb = _col8(L0_ffn_bn_b)
    Wr = np.asarray(L0_ffn_Wr, np.float32)
    wr = np.ascontiguousarray(Wr.reshape(8, 128, 8).transpose(1, 0, 2).reshape(128, 64))
    br = np.asarray(L0_ffn_br, np.float32).reshape(1, 8)
    Wd = np.asarray(L0_ffn_W_down, np.float32)
    wd = np.ascontiguousarray(Wd.reshape(8, 8, 128, 64).transpose(2, 0, 1, 3).reshape(128, 4096))
    bd = np.ascontiguousarray(np.asarray(L0_ffn_b_down, np.float32))
    Wu = np.asarray(L0_ffn_W_up, np.float32)
    wu = np.ascontiguousarray(Wu.transpose(1, 0, 2).reshape(64, 8192))
    bu = np.asarray(L0_ffn_b_up, np.float32)
    bu_col = np.ascontiguousarray(bu.T.reshape(8, 128, 8).transpose(1, 2, 0).reshape(128, 64))

    nc = build_nc()
    in_maps = []
    for b in range(8):
        xb_t = x[b].T.reshape(KF, 128, NCH, C).transpose(1, 2, 0, 3)
        in_maps.append({
            "xt": np.ascontiguousarray(xb_t),
            "wqkvg": wqkvg, "wproj": Wproj, "tabs": tabs, "ident_d": ident,
            "tri_d": tri, "iota_d": iota, "decay_d": decay, "lng_d": lng,
            "lnb_d": lnb, "bnm_d": bnm, "bnv_d": bnv, "bng_d": bng, "bnb_d": bnb,
            "wr": wr, "br_d": br, "wd_d": wd, "bd_d": bd, "wu_d": wu, "bu_d": bu_col,
        })
    res = run_bass_kernel_spmd(nc, in_maps, core_ids=list(range(8)))
    out = np.empty((8, T, D), np.float32)
    for b in range(8):
        a = np.asarray(res.results[b]["outT"], np.float32)
        out[b] = a.transpose(2, 0, 1, 3).reshape(D, T).T
    return out


# version 50
# speedup vs baseline: 1.0328x; 1.0328x over previous
import os
import numpy as np
import ml_dtypes
import concourse.bass as bass
import concourse.mybir as mybir
from concourse.bass_utils import run_bass_kernel_spmd

F32 = mybir.dt.float32
BF16 = mybir.dt.bfloat16
AF = mybir.ActivationFunctionType
ALU = mybir.AluOpType
AX = mybir.AxisListType

T, D, HH = 4096, 1024, 8
C = 256
NCH = T // C
MT = C // 128
KF = D // 128
NQKVG = 6144
EPS = 1e-5

ENGS = ("sync", "gpsimd", "tensor", "vector", "scalar")
ESEM = {"sync": "sx", "gpsimd": "gp", "tensor": "pe", "vector": "dv", "scalar": "ac"}
DSEM = {"sync": "dx", "gpsimd": "dw", "scalar": "da"}
KLVL = int(os.environ.get("KLVL", "7"))
KNCH = int(os.environ.get("KNCH", str(NCH)))


def _iv(ap):
    t = ap.tensor.name
    off = ap.offset
    reach = 1
    for s, c in list(ap.ap):
        reach += (int(c) - 1) * abs(int(s))
    sz = mybir.dt.size(ap.dtype)
    return (t, int(off) * sz, (int(off) + reach) * sz)


class Sched:

    def __init__(self):
        self.ops = []
        self.recs = {}

    def __init_lanes(self):
        pass

    def op(self, eng, fn, reads=(), writes=(), dma=False):
        oid = len(self.ops)
        deps = set()
        ivr = [_iv(a) for a in reads]
        ivw = [_iv(a) for a in writes]
        if dma:
            if not hasattr(self, "lane_idx"):
                self.lane_idx = {}
            ln = self.lane_idx.get(eng, 0)
            self.lane_idx[eng] = (ln + 1) % self.NLANE
            ekey = (eng, ln)
        else:
            ln = None
            ekey = eng

        for (t, lo, hi) in ivr:
            m = self.recs.get(t)
            if m:
                for (kind, peng), rec in m.items():
                    if kind == "w" and rec[0] < hi and lo < rec[1]:
                        deps.add(rec[2])
        for (t, lo, hi) in ivw:
            m = self.recs.get(t)
            if m:
                for (kind, peng), rec in m.items():
                    if rec[0] < hi and lo < rec[1]:
                        deps.add(rec[2])
        for (t, lo, hi) in ivr:
            m = self.recs.setdefault(t, {})
            rec = m.get(("r", ekey))
            m[("r", ekey)] = [lo if rec is None else min(lo, rec[0]),
                              hi if rec is None else max(hi, rec[1]), oid]
        for (t, lo, hi) in ivw:
            m = self.recs.setdefault(t, {})
            rec = m.get(("w", ekey))
            m[("w", ekey)] = [lo if rec is None else min(lo, rec[0]),
                              hi if rec is None else max(hi, rec[1]), oid]
        deps = {d for d in deps
                if not (eng == "tensor" and self.ops[d]["eng"] == "tensor")}
        self.ops.append({"eng": eng, "fn": fn, "dma": dma, "deps": deps, "lane": ln})
        return oid

    def dma(self, eng, fn, reads=(), writes=()):
        return self.op(eng, fn, reads, writes, dma=True)

    NLANE = 8

    def emit(self, nc):
        needed = set()
        for o in self.ops:
            needed.update(o["deps"])
        val = {}
        cnt = {e: 0 for e in ENGS}
        lane_cnt = {}
        pre_wait = {}
        for i, o in enumerate(self.ops):
            e = o["eng"]
            if o["dma"]:
                sname = f"{DSEM[e]}{o['lane']}"
                u = lane_cnt.get(sname, 0)
                if u:
                    pre_wait[i] = (sname, 16 * u)
                lane_cnt[sname] = u + 1
                val[i] = (sname, 16 * (u + 1))
            elif i in needed:
                cnt[e] += 1
                val[i] = (ESEM[e], cnt[e])
        q = {e: [] for e in ENGS}
        watermark = {}
        for i, o in enumerate(self.ops):
            e = o["eng"]
            per = {}
            for d in o["deps"]:
                s, v = val[d]
                per[s] = max(per.get(s, 0), v)
            if i in pre_wait:
                s, v = pre_wait[i]
                per[s] = max(per.get(s, 0), v)
            for s in sorted(per):
                v = per[s]
                if watermark.get((e, s), 0) >= v:
                    continue
                watermark[(e, s)] = v
                q[e].append(("w", s, v))
            if i in val:
                q[e].append(("i", o["fn"], val[i][0], 16 if o["dma"] else 1))
            else:
                q[e].append(("o", o["fn"]))

        import contextlib
        with contextlib.ExitStack() as stack:
            SM = {}
            names = ["pe", "dv", "ac", "gp", "sx"]
            names += [f"dx{i}" for i in range(self.NLANE)]
            names += [f"dw{i}" for i in range(self.NLANE)]
            names += [f"da{i}" for i in range(self.NLANE)]
            for name in names:
                SM[name] = stack.enter_context(nc.semaphore(name))
            block = stack.enter_context(nc.Block())

            def run_queue(E, items, tail=()):
                for it in items:
                    if it[0] == "w":
                        E.wait_ge(SM[it[1]], it[2])
                    elif it[0] == "i":
                        it[1](E).then_inc(SM[it[2]], it[3])
                    else:
                        it[1](E)
                for s, v in tail:
                    if v:
                        E.wait_ge(SM[s], v)

            tail = [(s, v * 16) for s, v in
                    ((k, lane_cnt.get(k, 0)) for k in names if k[0] == "d") if v]

            @block.sync
            def _(E):
                run_queue(E, q["sync"], tail)

            @block.gpsimd
            def _(E):
                run_queue(E, q["gpsimd"])

            @block.tensor
            def _(E):
                run_queue(E, q["tensor"])

            @block.vector
            def _(E):
                run_queue(E, q["vector"])

            @block.scalar
            def _(E):
                run_queue(E, q["scalar"])
        return nc


def build_nc():
    nc = bass.Bass("TRN2", target_bir_lowering=False)

    d_xt = nc.dram_tensor("xt", [128, NCH, KF, C], F32, kind="ExternalInput").ap()
    d_wqkvg = nc.dram_tensor("wqkvg", [D, NQKVG], F32, kind="ExternalInput").ap()
    d_wproj = nc.dram_tensor("wproj", [2048, D], F32, kind="ExternalInput").ap()
    d_tabs = nc.dram_tensor("tabs", [128, NCH, MT, 2048], BF16, kind="ExternalInput").ap()
    d_ident = nc.dram_tensor("ident_d", [128, 128], BF16, kind="ExternalInput").ap()
    d_tri = nc.dram_tensor("tri_d", [128, 128], BF16, kind="ExternalInput").ap()
    d_iota = nc.dram_tensor("iota_d", [1, C], F32, kind="ExternalInput").ap()
    d_decay = nc.dram_tensor("decay_d", [1, 8], F32, kind="ExternalInput").ap()
    d_lng = nc.dram_tensor("lng_d", [128, 8], F32, kind="ExternalInput").ap()
    d_lnb = nc.dram_tensor("lnb_d", [128, 8], F32, kind="ExternalInput").ap()
    d_bnm = nc.dram_tensor("bnm_d", [128, 8], F32, kind="ExternalInput").ap()
    d_bnv = nc.dram_tensor("bnv_d", [128, 8], F32, kind="ExternalInput").ap()
    d_bng = nc.dram_tensor("bng_d", [128, 8], F32, kind="ExternalInput").ap()
    d_bnb = nc.dram_tensor("bnb_d", [128, 8], F32, kind="ExternalInput").ap()
    d_wr = nc.dram_tensor("wr", [128, 64], F32, kind="ExternalInput").ap()
    d_br = nc.dram_tensor("br_d", [1, 8], F32, kind="ExternalInput").ap()
    d_wd = nc.dram_tensor("wd_d", [128, 4096], F32, kind="ExternalInput").ap()
    d_bd = nc.dram_tensor("bd_d", [8, 64], F32, kind="ExternalInput").ap()
    d_wu = nc.dram_tensor("wu_d", [64, 8192], F32, kind="ExternalInput").ap()
    d_bu = nc.dram_tensor("bu_d", [128, 64], F32, kind="ExternalInput").ap()

    d_wqkvg_bf = nc.dram_tensor("wqkvg_bf", [128, 12, KF, 512], BF16, kind="Internal").ap()
    d_wproj_bf = nc.dram_tensor("wproj_bf", [2048, D], BF16, kind="Internal").ap()
    d_xp = nc.dram_tensor("xp", [128, NCH, KF, C], F32, kind="Internal").ap()
    d_zt = nc.dram_tensor("zt", [128, NCH, KF, C], BF16, kind="Internal").ap()
    d_out = nc.dram_tensor("outT", [128, NCH, KF, C], F32, kind="ExternalOutput").ap()

    A = nc.alloc_sbuf_tensor
    x_sb = [A(f"x{i}", [128, KF, C], F32) for i in range(3)]
    tab_sb = [A(f"tab{i}", [128, MT, 2048], BF16) for i in range(2)]
    wnb_sb = [A(f"wnb{i}", [128, 4, 512], BF16)[:] for i in range(8)]
    wp_sb = [A(f"wp{i}", [128, 2, D], BF16) for i in range(4)]
    tmp_f = A("tmp_f", [128, KF, C], F32)
    lnx_sb = A("lnx_sb", [128, KF, C], BF16)
    rows = A("rows", [1, 1792], F32)
    qbf = [A(f"qbf{i}", [128, 1024], BF16) for i in range(2)]
    rt1 = A("rt1", [128, 512], BF16)
    rt2 = A("rt2", [128, 512], BF16)
    ck_sb = A("ck_sb", [128, 512], BF16)
    sk_sb = A("sk_sb", [128, 512], BF16)
    qs_sb = A("qs_sb", [128, MT, 1024], BF16)
    ks_sb = A("ks_sb", [128, MT, 1024], BF16)
    xbf_fl = qs_sb[:].rearrange("p m n -> p (m n)")
    xbf_sb = xbf_fl.rearrange("p (k t) -> p k t", t=C)
    qT_sb = A("qT_sb", [128, HH * C], BF16)
    kT_sb = A("kT_sb", [128, HH * C], BF16)
    v_sb = A("v_sb", [128, MT, 2048], BF16)
    gate_sb = A("gate_sb", [128, MT, 2048], BF16)
    sT_sb = A("sT8", [128, HH, 384], BF16)
    ao_sb = A("ao_sb", [128, MT, 2048], BF16)
    xsq_fl = ao_sb[:].rearrange("p m n -> p (m n)")[:, 0:KF * C]
    xsq_sb = xsq_fl.rearrange("p (k t) -> p k t", t=C)
    gst = A("gst", [128, 96], F32)
    gg_sb = A("gg_sb", [128, MT, 2048], BF16)
    rows_b = gg_sb[:].rearrange("p m n -> p (m n)")[0:1, 0:3584].bitcast(F32)
    ggT_sb = A("ggT_sb", [128, 16, C], BF16)
    st_bf = A("st_bf", [128, HH * 256], BF16)
    st_tmp = A("st_tmp", [128, 256], F32)
    xp_sb = A("xp_sb0", [128, KF, C], F32)
    bc_sb = A("bc_sb", [128, 512], F32)
    pe_sc = [A(f"pe_sc{i}", [128, 256], F32) for i in range(2)]
    out_sb = A("out_sb", [128, KF, C], F32)
    zbuf = [tmp_f[:].rearrange("p k t -> p (k t)").bitcast(BF16)[
        :, i * KF * C:(i + 1) * KF * C].rearrange("p (k t) -> p k t", t=C)
        for i in range(2)]
    sc_sb = A("sc_sb", [128, 256], F32)
    ident = A("ident", [128, 128], BF16)
    tri = A("tri", [128, 128], BF16)
    iota = A("iota", [1, C], F32)
    decay = A("decay", [1, 8], F32)
    ldrow = A("ldrow", [1, 8], F32)
    dcrow = A("dcrow", [1, 8], F32)
    dc_bc = A("dc_bc", [128, 8], F32)
    dnegj = A("dnegj", [128, MT, 8], F32)
    d2i = A("d2i", [128, MT, 8], F32)
    d1i = A("d1i", [128, MT, 8], F32)
    ones_bf = A("ones_bf", [128, 1], BF16)
    eps_c = A("eps_c", [128, 1], F32)
    ones_f = A("ones_f", [1, 128], F32)
    ones_c = A("ones_c", [128, 1], F32)
    lng = A("lng", [128, 8], F32)
    lnb = A("lnb", [128, 8], F32)
    bnm = A("bnm", [128, 8], F32)
    bnv = A("bnv", [128, 8], F32)
    bng = A("bng", [128, 8], F32)
    bnb = A("bnb", [128, 8], F32)
    wr_sb = A("wr_sb", [128, 64], F32)
    br_sb = A("br_sb", [1, 8], F32)
    bd_sb = A("bd_sb", [8, 64], F32)
    bu_sb = A("bu_sb", [128, 64], F32)
    we_sb = [wnb_sb[i].rearrange("p k n -> p (k n)").bitcast(F32)[:, 0:512] for i in range(2)]
    wue_sb = [tab_sb[i][:].rearrange("p m q -> p (m q)").bitcast(F32)[0:64, 0:1024] for i in range(2)]
    acc = [qbf[0][:].bitcast(F32), qbf[1][:].bitcast(F32)]
    uacc = [gate_sb[0:64, 0, :].bitcast(F32), gate_sb[0:64, 1, :].bitcast(F32)]
    wd_bf = qs_sb[:, 0, 0:512]
    wu_bf = ks_sb[0:64, 0, :]
    ysent = A("ysent", [128, 8], F32)
    zs_sb = A("zs_sb", [128, 8], F32)
    zb_sb = A("zb_sb", [128, 8], F32)
    sm_sb = A("sm_sb", [1, 32], F32)
    ohT = A("ohT", [8, 1], F32)
    ohbc = A("ohbc", [128, 8], F32)
    gatebc = A("gatebc", [128, 1], F32)
    biash = A("biash", [64, 1], F32)
    bug = [A("bug_a", [128, 8], F32), A("bug_b", [128, 8], F32)]
    h_bf2 = [A("h_bf0", [64, 256], BF16)[:],
             tab_sb[0][:].rearrange("p m q -> p (m q)")[0:64, 0:256]]

    PA = nc.alloc_psum_tensor("PA", [128, 1024], F32)
    PB = nc.alloc_psum_tensor("PB", [128, 1024], F32)
    PC = nc.alloc_psum_tensor("PC", [128, 512], F32)
    PD = nc.alloc_psum_tensor("PD", [128, 512], F32)
    PE_ = nc.alloc_psum_tensor("PE_", [128, 512], F32)
    PT = nc.alloc_psum_tensor("PT", [128, 1024], BF16)

    S = Sched()

    tg_ctr = [0]

    def trans_group(srcs, dst_view, drain="vector"):
        n = len(srcs)
        assert n <= 8
        PTx = PT[:] if tg_ctr[0] % 2 == 0 else PD[:].bitcast(BF16)
        tg_ctr[0] += 1
        for j, sap in enumerate(srcs):
            S.op("tensor", lambda E, s=sap, j=j, PTx=PTx: E.transpose(
                PTx[:, j * 128:(j + 1) * 128], s, ident[:]),
                 reads=[sap, ident[:]], writes=[PTx[:, j * 128:(j + 1) * 128]])
        pv = PTx.rearrange("p (j n) -> p j n", n=128)[:, 0:n, :]
        if drain == "vector":
            S.op("vector", lambda E, d=dst_view, pv=pv: E.tensor_copy(d, pv),
                 reads=[pv], writes=[dst_view])
        else:
            S.op("scalar", lambda E, d=dst_view, pv=pv: E.activation(d, pv, AF.Copy),
                 reads=[pv], writes=[dst_view])

    for t_, dsrc in [(ident, d_ident), (tri, d_tri), (iota, d_iota), (decay, d_decay),
                     (lng, d_lng), (lnb, d_lnb), (bnm, d_bnm), (bnv, d_bnv),
                     (bng, d_bng), (bnb, d_bnb), (wr_sb, d_wr), (br_sb, d_br),
                     (bd_sb, d_bd), (bu_sb, d_bu)]:
        S.dma("sync", lambda E, a=t_, b=dsrc: E.dma_start(out=a[:], in_=b),
              reads=[dsrc], writes=[t_[:]])

    for nb in range(12):
        iv_ = d_wqkvg.rearrange("(k p) n -> p k n", p=128)[:, :, nb * 512:(nb + 1) * 512]
        ov_ = d_wqkvg_bf[:, nb]
        S.dma("gpsimd", lambda E, iv_=iv_, ov_=ov_: E.dma_start(out=ov_, in_=iv_),
              reads=[iv_], writes=[ov_])
    for i in range(16):
        S.dma("gpsimd", lambda E, i=i: E.dma_start(
            out=d_wproj_bf[i * 128:(i + 1) * 128, :],
            in_=d_wproj[i * 128:(i + 1) * 128, :]),
            reads=[d_wproj[i * 128:(i + 1) * 128, :]],
            writes=[d_wproj_bf[i * 128:(i + 1) * 128, :]])

    for t_, v_ in [(ones_bf, 1.0), (eps_c, EPS), (ones_f, 1.0), (ones_c, 1.0),
                   (st_bf, 0.0), (ysent, 0.0)]:
        S.op("vector", lambda E, t=t_, v=v_: E.memset(t[:], v), writes=[t_[:]])

    S.op("scalar", lambda E: E.activation(ldrow[:], decay[:], AF.Ln),
         reads=[decay[:]], writes=[ldrow[:]])
    S.op("scalar", lambda E: E.activation(dcrow[:], ldrow[:], AF.Exp, scale=float(C)),
         reads=[ldrow[:]], writes=[dcrow[:]])
    for m in range(MT):
        S.op("tensor", lambda E, m=m: E.matmul(
            PC[:, 0:8], iota[:, m * 128:(m + 1) * 128], ldrow[:], start=True, stop=True),
            reads=[iota[:, m * 128:(m + 1) * 128], ldrow[:]], writes=[PC[:, 0:8]])
        for dst, sc in [(dnegj, -1.0), (d1i, 1.0), (d2i, 2.0)]:
            S.op("scalar", lambda E, m=m, d=dst, sc=sc: E.activation(
                d[:, m, :], PC[:, 0:8], AF.Exp, scale=sc),
                reads=[PC[:, 0:8]], writes=[dst[:, m, :]])
    S.op("tensor", lambda E: E.matmul(PC[:, 8:16], ones_f[:], dcrow[:], start=True, stop=True),
         reads=[ones_f[:], dcrow[:]], writes=[PC[:, 8:16]])
    S.op("vector", lambda E: E.tensor_copy(dc_bc[:], PC[:, 8:16]),
         reads=[PC[:, 8:16]], writes=[dc_bc[:]])

    def ln_cast_half(xbuf, half):
        a, b = half * (KF // 2) * C, (half + 1) * (KF // 2) * C
        xv = xbuf[:, half * (KF // 2):(half + 1) * (KF // 2), :]
        xf = xbuf[:].rearrange("p k t -> p (k t)")[:, a:b]
        S.op("scalar", lambda E, xf=xf, a=a, b=b: E.activation(
            xbf_fl[:, a:b], xf, AF.Copy),
            reads=[xv], writes=[xbf_fl[:, a:b]])
        S.op("vector", lambda E, a=a, b=b: E.tensor_tensor(
            xsq_fl[:, a:b], xbf_fl[:, a:b], xbf_fl[:, a:b], ALU.mult),
            reads=[xbf_fl[:, a:b]], writes=[xsq_fl[:, a:b]])

    def ln_cast(xbuf):
        ln_cast_half(xbuf, 0)
        ln_cast_half(xbuf, 1)

    def ln_stats_mms(k0, k1):
        for src, off in ((xbf_sb, 0), (xsq_sb, C)):
            for k in range(k0, k1):
                S.op("tensor", lambda E, k=k, s=src, off=off: E.matmul(
                    PC[0:1, off:off + C], ones_bf[:], s[:, k, :],
                    start=(k == 0), stop=(k == KF - 1)),
                    reads=[ones_bf[:], src[:, k, :]], writes=[PC[0:1, off:off + C]])

    def ln_stats_rows(rows, combined):
        S.op("scalar", lambda E: E.activation(rows[:, 512:1024], PC[0:1, 0:512], AF.Copy,
                                              scale=1.0 / D),
             reads=[PC[0:1, 0:512]], writes=[rows[:, 512:1024]])
        S.op("vector", lambda E: E.tensor_tensor(rows[:, 1024:1280], rows[:, 512:768],
                                                 rows[:, 512:768], ALU.mult),
             reads=[rows[:, 512:768]], writes=[rows[:, 1024:1280]])
        S.op("vector", lambda E: E.tensor_sub(rows[:, 1280:1536], rows[:, 768:1024],
                                              rows[:, 1024:1280]),
             reads=[rows[:, 768:1280]], writes=[rows[:, 1280:1536]])
        S.op("scalar", lambda E: E.activation(rows[:, 1536:1792], rows[:, 1280:1536],
                                              AF.Ln, bias=eps_c[0:1, 0:1]),
             reads=[rows[:, 1280:1536], eps_c[0:1, 0:1]], writes=[rows[:, 1536:1792]])
        S.op("scalar", lambda E: E.activation(rows[:, 0:256], rows[:, 1536:1792],
                                              AF.Exp, scale=-0.5),
             reads=[rows[:, 1536:1792]], writes=[rows[:, 0:256]])
        if combined:
            S.op("scalar", lambda E: E.activation(rows[:, 1024:1280], rows[:, 1536:1792],
                                                  AF.Exp, scale=-1.0),
                 reads=[rows[:, 1536:1792]], writes=[rows[:, 1024:1280]])
            S.op("vector", lambda E: E.tensor_tensor(rows[:, 1024:1280], rows[:, 1280:1536],
                                                     rows[:, 1024:1280], ALU.mult),
                 reads=[rows[:, 1280:1536], rows[:, 1024:1280]], writes=[rows[:, 1024:1280]])
            S.op("scalar", lambda E: E.activation(rows[:, 1024:1280], rows[:, 1024:1280],
                                                  AF.Ln, bias=eps_c[0:1, 0:1]),
                 reads=[rows[:, 1024:1280], eps_c[0:1, 0:1]], writes=[rows[:, 1024:1280]])
            S.op("vector", lambda E: E.tensor_add(rows[:, 1024:1280], rows[:, 1024:1280],
                                                  rows[:, 1536:1792]),
                 reads=[rows[:, 1024:1280], rows[:, 1536:1792]], writes=[rows[:, 1024:1280]])
            S.op("scalar", lambda E: E.activation(rows[:, 0:256], rows[:, 1024:1280],
                                                  AF.Exp, scale=-0.5),
                 reads=[rows[:, 1024:1280]], writes=[rows[:, 0:256]])
        S.op("vector", lambda E: E.tensor_tensor(rows[:, 256:512], rows[:, 512:768],
                                                 rows[:, 0:256], ALU.mult),
             reads=[rows[:, 512:768], rows[:, 0:256]], writes=[rows[:, 256:512]])
    def ln_apply(xbuf, dst, rows, Pv=None):
        if Pv is None:
            Pv = PA[:, 0:512]
        S.op("tensor", lambda E, Pv=Pv: E.matmul(Pv, ones_f[:], rows[:, 0:512],
                                                 start=True, stop=True),
             reads=[ones_f[:], rows[:, 0:512]], writes=[Pv])
        rbc = Pv[:, 0:256].unsqueeze(1).broadcast_to([128, KF, C])
        mbc = Pv[:, 256:512].unsqueeze(1).broadcast_to([128, KF, C])
        S.op("vector", lambda E, xb=xbuf, rbc=rbc: E.tensor_tensor(
            tmp_f[:], xb[:], rbc, ALU.mult),
            reads=[xbuf[:], Pv[:, 0:256]], writes=[tmp_f[:]])
        S.op("vector", lambda E, dst=dst, mbc=mbc: E.tensor_tensor(
            dst[:], tmp_f[:], mbc, ALU.subtract),
            reads=[tmp_f[:], Pv[:, 256:512]], writes=[dst[:]])

    def load_x(c, dest, src_dram):
        xv = src_dram[:, c]
        S.dma("scalar", lambda E, o=dest, i=xv: E.dma_start(out=o[:], in_=i),
              reads=[xv], writes=[dest[:]])

    def load_tab(c):
        tv = d_tabs[:, c]
        S.dma("scalar", lambda E, o=tab_sb[c % 2], i=tv: E.dma_start(out=o[:], in_=i),
              reads=[tv], writes=[tab_sb[c % 2][:]])

    wnb_ctr = [0]

    def load_wnb_half(nb, half, q="sync"):
        buf = wnb_sb[wnb_ctr[0] % 8]
        wnb_ctr[0] += 1
        wv = d_wqkvg_bf[:, nb, half * 4:(half + 1) * 4, :]
        S.dma(q, lambda E, o=buf, i=wv: E.dma_start(out=o, in_=i),
              reads=[wv], writes=[buf])
        return buf

    def load_wp(vtp):
        iv_ = d_wproj_bf[vtp * 256:(vtp + 1) * 256, :].rearrange(
            "(two p) n -> p two n", p=128)
        S.dma("sync", lambda E, o=wp_sb[vtp % 4], i=iv_: E.dma_start(out=o[:], in_=i),
              reads=[d_wproj_bf[vtp * 256:(vtp + 1) * 256, :]], writes=[wp_sb[vtp % 4][:]])

    qk_bufs = {}
    preloaded = set()

    def preload(c):
        if c in preloaded or c >= NCH:
            return
        preloaded.add(c)
        load_x(c, x_sb[c % 3], d_xt)
        load_tab(c)

    def load_pair_bufs(nb0, q2="sync"):
        return {(0, 0): load_wnb_half(nb0, 0), (1, 0): load_wnb_half(nb0 + 1, 0, q2),
                (0, 1): load_wnb_half(nb0, 1), (1, 1): load_wnb_half(nb0 + 1, 1, q2)}

    def qk_pre(c):
        for nb0 in (0, 2):
            qk_bufs[(c, nb0)] = load_pair_bufs(nb0)

    def ln_stats(rows, combined):
        ln_stats_mms(0, KF)
        ln_stats_rows(rows, combined)

    def front_a(c):
        ln_cast(x_sb[c % 3])
        ln_stats(rows, combined=False)

    def front_b(c):
        ln_apply(x_sb[c % 3], lnx_sb, rows, Pv=PT[:].bitcast(F32))

    def qk_phase(c):
        tb = tab_sb[c % 2]

        def qk_mms(nb0):
            psums = [PA, PB]
            if (c, nb0) in qk_bufs:
                bufs = qk_bufs.pop((c, nb0))
            else:
                bufs = load_pair_bufs(nb0)
            for k in range(KF):
                for m in range(MT):
                    P = psums[m]
                    for nbo in range(2):
                        wb = bufs[(nbo, k // 4)]
                        S.op("tensor", lambda E, m=m, nbo=nbo, k=k, P=P, wb=wb: E.matmul(
                            P[:, nbo * 512:(nbo + 1) * 512],
                            lnx_sb[:, k, m * 128:(m + 1) * 128],
                            wb[:, k % 4, :],
                            start=(k == 0), stop=(k == KF - 1)),
                            reads=[lnx_sb[:, k, m * 128:(m + 1) * 128], wb[:, k % 4, :]],
                            writes=[P[:, nbo * 512:(nbo + 1) * 512]])

        def rotary(psums, dst, fold_k):
            for m in range(MT):
                P, q = psums[m], qbf[m]
                S.op("scalar", lambda E, P=P, q=q: E.activation(q[:], P[:], AF.Copy),
                     reads=[P[:]], writes=[q[:]])
            for m in range(MT):
                q = qbf[m]
                qv = q[:].rearrange("p (h two r) -> p two h r", two=2, r=64)
                qe, qo = qv[:, 0], qv[:, 1]
                if fold_k:
                    dneg_v = dnegj[:, m, :].unsqueeze(2).broadcast_to([128, 8, 64])
                    for dstt, toff in ((ck_sb, 1024), (sk_sb, 1536)):
                        tv = tb[:, m, toff:toff + 512].rearrange("p (h r) -> p h r", r=64)
                        S.op("vector", lambda E, d=dstt, tv=tv, dv_=dneg_v: E.tensor_tensor(
                            d[:].rearrange("p (h r) -> p h r", r=64), tv, dv_, ALU.mult),
                            reads=[tb[:, m, toff:toff + 512], dnegj[:, m, :]],
                            writes=[dstt[:]])
                    cs = ck_sb[:].rearrange("p (h r) -> p h r", r=64)
                    ss = sk_sb[:].rearrange("p (h r) -> p h r", r=64)
                    cs_r, ss_r = ck_sb[:], sk_sb[:]
                else:
                    cs = tb[:, m, 0:512].rearrange("p (h r) -> p h r", r=64)
                    ss = tb[:, m, 512:1024].rearrange("p (h r) -> p h r", r=64)
                    cs_r, ss_r = tb[:, m, 0:512], tb[:, m, 512:1024]
                t1v = rt1[:].rearrange("p (h r) -> p h r", r=64)
                t2v = rt2[:].rearrange("p (h r) -> p h r", r=64)
                dvw = dst[:, m, :].rearrange("p (h two r) -> p two h r", two=2, r=64)
                de, do = dvw[:, 0], dvw[:, 1]
                S.op("vector", lambda E, a=qe, b=cs, t1v=t1v: E.tensor_tensor(
                    t1v, a, b, ALU.mult), reads=[q[:], cs_r], writes=[rt1[:]])
                S.op("vector", lambda E, a=qo, b=ss, t2v=t2v: E.tensor_tensor(
                    t2v, a, b, ALU.mult), reads=[q[:], ss_r], writes=[rt2[:]])
                S.op("vector", lambda E, o=de, t1v=t1v, t2v=t2v: E.tensor_tensor(
                    o, t1v, t2v, ALU.subtract),
                    reads=[rt1[:], rt2[:]], writes=[dst[:, m, :]])
                S.op("vector", lambda E, a=qo, b=cs, t1v=t1v: E.tensor_tensor(
                    t1v, a, b, ALU.mult), reads=[q[:], cs_r], writes=[rt1[:]])
                S.op("vector", lambda E, a=qe, b=ss, t2v=t2v: E.tensor_tensor(
                    t2v, a, b, ALU.mult), reads=[q[:], ss_r], writes=[rt2[:]])
                S.op("vector", lambda E, o=do, t1v=t1v, t2v=t2v: E.tensor_tensor(
                    o, t1v, t2v, ALU.add),
                    reads=[rt1[:], rt2[:]], writes=[dst[:, m, :]])

        qk_mms(0)
        rotary([PA, PB], qs_sb, False)
        qk_mms(2)
        rotary([PA, PB], ks_sb, True)

    def vgate_phase(c):
        def vg_pair(nb0, dst, act):
            slots = {(0, 0): PC[:], (0, 1): PD[:],
                     (1, 0): PA[:, 0:512], (1, 1): PA[:, 512:1024]}
            bufs = load_pair_bufs(nb0, q2="gpsimd")
            for k in range(KF):
                for m in range(MT):
                    for nbo in range(2):
                        P = slots[(m, nbo)]
                        wb = bufs[(nbo, k // 4)]
                        S.op("tensor", lambda E, m=m, k=k, P=P, wb=wb: E.matmul(
                            P, lnx_sb[:, k, m * 128:(m + 1) * 128], wb[:, k % 4, :],
                            start=(k == 0), stop=(k == KF - 1)),
                            reads=[lnx_sb[:, k, m * 128:(m + 1) * 128], wb[:, k % 4, :]],
                            writes=[P])
            for m in range(MT):
                for nbo in range(2):
                    nb = nb0 + nbo
                    off = (nb - (4 if dst is v_sb else 8)) * 512
                    P = slots[(m, nbo)]
                    S.op("scalar", lambda E, m=m, off=off, P=P, dst=dst, act=act: E.activation(
                        dst[:, m, off:off + 512], P, act),
                        reads=[P], writes=[dst[:, m, off:off + 512]])
        vg_pair(4, v_sb, AF.Copy)
        vg_pair(6, v_sb, AF.Copy)
        vg_pair(8, gate_sb, AF.Silu)
        vg_pair(10, gate_sb, AF.Silu)

    def qkt_phase(c):
        for m in range(MT):
            for src_, dstT in ((qs_sb, qT_sb), (ks_sb, kT_sb)):
                srcs = [src_[:, m, h * 128:(h + 1) * 128] for h in range(HH)]
                dv_ = dstT[:].rearrange("p (h n) -> p h n", n=128)[:, m::MT, :]
                trans_group(srcs, dv_, drain="scalar")

    def retention_phase(c):
        def scores(h):
            PS = PE_ if h % 2 == 0 else PA
            S.op("tensor", lambda E, h=h, PS=PS: E.matmul(
                PS[:, 0:256], kT_sb[:, h * C:h * C + 128], qT_sb[:, h * C:h * C + 256],
                start=True, stop=True),
                reads=[kT_sb[:, h * C:h * C + 128], qT_sb[:, h * C:h * C + 256]],
                writes=[PS[:, 0:256]])
            S.op("tensor", lambda E, h=h, PS=PS: E.matmul(
                PS[:, 256:384], kT_sb[:, h * C + 128:h * C + 256],
                qT_sb[:, h * C + 128:h * C + 256], start=True, stop=True),
                reads=[kT_sb[:, h * C + 128:h * C + 256], qT_sb[:, h * C + 128:h * C + 256]],
                writes=[PS[:, 256:384]])

        def sevac(h):
            PS = PE_ if h % 2 == 0 else PA
            sT = sT_sb[:, h, :]
            S.op("vector", lambda E, sT=sT, PS=PS: E.tensor_tensor(
                sT[:, 0:128], PS[:, 0:128], tri[:], ALU.mult),
                reads=[PS[:, 0:128], tri[:]], writes=[sT[:, 0:128]])
            S.op("vector", lambda E, sT=sT, PS=PS: E.tensor_tensor(
                sT[:, 256:384], PS[:, 256:384], tri[:], ALU.mult),
                reads=[PS[:, 256:384], tri[:]], writes=[sT[:, 256:384]])
            S.op("scalar", lambda E, sT=sT, PS=PS: E.activation(
                sT[:, 128:256], PS[:, 128:256], AF.Copy),
                reads=[PS[:, 128:256]], writes=[sT[:, 128:256]])

        def inner(h):
            PO = PD if h % 2 == 0 else PB
            sT = sT_sb[:, h, :]
            S.op("tensor", lambda E, h=h, sT=sT, PO=PO: E.matmul(
                PO[:, 0:256], sT[:, 0:128], v_sb[:, 0, h * 256:(h + 1) * 256],
                start=True, stop=False),
                reads=[sT[:, 0:128], v_sb[:, 0, h * 256:(h + 1) * 256]],
                writes=[PO[:, 0:256]])
            S.op("tensor", lambda E, h=h, PO=PO: E.matmul(
                PO[:, 0:256], qT_sb[:, h * C:h * C + 128], st_bf[:, h * 256:(h + 1) * 256],
                start=False, stop=True),
                reads=[qT_sb[:, h * C:h * C + 128], st_bf[:, h * 256:(h + 1) * 256]],
                writes=[PO[:, 0:256]])
            S.op("tensor", lambda E, h=h, sT=sT, PO=PO: E.matmul(
                PO[:, 256:512], sT[:, 128:256], v_sb[:, 0, h * 256:(h + 1) * 256],
                start=True, stop=False),
                reads=[sT[:, 128:256], v_sb[:, 0, h * 256:(h + 1) * 256]],
                writes=[PO[:, 256:512]])
            S.op("tensor", lambda E, h=h, sT=sT, PO=PO: E.matmul(
                PO[:, 256:512], sT[:, 256:384], v_sb[:, 1, h * 256:(h + 1) * 256],
                start=False, stop=False),
                reads=[sT[:, 256:384], v_sb[:, 1, h * 256:(h + 1) * 256]],
                writes=[PO[:, 256:512]])
            S.op("tensor", lambda E, h=h, PO=PO: E.matmul(
                PO[:, 256:512], qT_sb[:, h * C + 128:h * C + 256],
                st_bf[:, h * 256:(h + 1) * 256], start=False, stop=True),
                reads=[qT_sb[:, h * C + 128:h * C + 256], st_bf[:, h * 256:(h + 1) * 256]],
                writes=[PO[:, 256:512]])
            for i in range(MT):
                S.op("scalar", lambda E, h=h, i=i, PO=PO: E.activation(
                    ao_sb[:, i, h * 256:(h + 1) * 256], PO[:, i * 256:(i + 1) * 256], AF.Copy),
                    reads=[PO[:, i * 256:(i + 1) * 256]],
                    writes=[ao_sb[:, i, h * 256:(h + 1) * 256]])

        def kv_mms(h):
            P = PC if h % 4 < 2 else PE_
            ko = (h % 2) * 256
            S.op("tensor", lambda E, h=h, ko=ko, P=P: E.matmul(
                P[:, ko:ko + 256], ks_sb[:, 0, h * 128:(h + 1) * 128],
                v_sb[:, 0, h * 256:(h + 1) * 256], start=True, stop=False),
                reads=[ks_sb[:, 0, h * 128:(h + 1) * 128], v_sb[:, 0, h * 256:(h + 1) * 256]],
                writes=[P[:, ko:ko + 256]])
            S.op("tensor", lambda E, h=h, ko=ko, P=P: E.matmul(
                P[:, ko:ko + 256], ks_sb[:, 1, h * 128:(h + 1) * 128],
                v_sb[:, 1, h * 256:(h + 1) * 256], start=False, stop=True),
                reads=[ks_sb[:, 1, h * 128:(h + 1) * 128], v_sb[:, 1, h * 256:(h + 1) * 256]],
                writes=[P[:, ko:ko + 256]])

        def kv_upd(h):
            P = PC if h % 4 < 2 else PE_
            ko = (h % 2) * 256
            S.op("vector", lambda E, h=h, ko=ko, P=P: E.tensor_tensor(
                st_tmp[:], P[:, ko:ko + 256],
                st_bf[:, h * 256:(h + 1) * 256], ALU.add),
                reads=[P[:, ko:ko + 256], st_bf[:, h * 256:(h + 1) * 256]],
                writes=[st_tmp[:]])
            S.op("vector", lambda E, h=h: E.tensor_scalar(
                st_bf[:, h * 256:(h + 1) * 256], st_tmp[:], dc_bc[:, h:h + 1], None, ALU.mult),
                reads=[st_tmp[:], dc_bc[:, h:h + 1]],
                writes=[st_bf[:, h * 256:(h + 1) * 256]])

        for h in range(HH):
            scores(h)
            sevac(h)
        for h in range(4):
            kv_mms(h)
        for h in range(HH):
            inner(h)
        for h in range(4):
            kv_upd(h)
        for h in range(4, HH):
            kv_mms(h)
        for h in range(4, HH):
            kv_upd(h)

    def gn_phase(c):
        aof = ao_sb[:].rearrange("p i n -> p (i n)")
        S.op("vector", lambda E, aof=aof: E.tensor_tensor(
            gg_sb[:].rearrange("p i n -> p (i n)"), aof, aof, ALU.mult),
            reads=[ao_sb[:]], writes=[gg_sb[:]])
        S.op("vector", lambda E: E.tensor_reduce(
            gst[:, 0:16], ao_sb[:].rearrange("p i (h n) -> p (i h) n", n=256),
            AX.X, ALU.add), reads=[ao_sb[:]], writes=[gst[:, 0:16]])
        S.op("vector", lambda E: E.tensor_reduce(
            gst[:, 16:32], gg_sb[:].rearrange("p i (h n) -> p (i h) n", n=256),
            AX.X, ALU.add), reads=[gg_sb[:]], writes=[gst[:, 16:32]])
        S.op("vector", lambda E: E.tensor_scalar(gst[:, 32:48], gst[:, 0:16],
                                                 -1.0 / 256, None, ALU.mult),
             reads=[gst[:, 0:16]], writes=[gst[:, 32:48]])
        S.op("vector", lambda E: E.tensor_scalar(gst[:, 64:80], gst[:, 16:32],
                                                 1.0 / 256, None, ALU.mult),
             reads=[gst[:, 16:32]], writes=[gst[:, 64:80]])
        S.op("vector", lambda E: E.tensor_tensor(gst[:, 80:96], gst[:, 32:48],
                                                 gst[:, 32:48], ALU.mult),
             reads=[gst[:, 32:48]], writes=[gst[:, 80:96]])
        S.op("vector", lambda E: E.tensor_sub(sc_sb[:, 0:16], gst[:, 64:80], gst[:, 80:96]),
             reads=[gst[:, 64:96]], writes=[sc_sb[:, 0:16]])
        S.op("vector", lambda E: E.tensor_tensor(
            gst[:, 64:80], sc_sb[:, 0:16], d2i[:].rearrange("p m h -> p (m h)"), ALU.mult),
            reads=[sc_sb[:, 0:16], d2i[:]], writes=[gst[:, 64:80]])
        S.op("scalar", lambda E: E.activation(gst[:, 80:96], gst[:, 64:80], AF.Ln,
                                              bias=eps_c[:]),
             reads=[gst[:, 64:80], eps_c[:]], writes=[gst[:, 80:96]])
        S.op("scalar", lambda E: E.activation(sc_sb[:, 16:32], gst[:, 80:96],
                                              AF.Exp, scale=-0.5),
             reads=[gst[:, 80:96]], writes=[sc_sb[:, 16:32]])
        S.op("vector", lambda E: E.tensor_tensor(
            gst[:, 48:64], sc_sb[:, 16:32], d1i[:].rearrange("p m h -> p (m h)"), ALU.mult),
            reads=[sc_sb[:, 16:32], d1i[:]], writes=[gst[:, 48:64]])
        for i in range(MT):
            for h in range(HH):
                col = i * 8 + h
                S.op("vector", lambda E, i=i, h=h, col=col: E.tensor_scalar(
                    sc_sb[:, 0:256], gate_sb[:, i, h * 256:(h + 1) * 256],
                    gst[:, 48 + col:49 + col], None, ALU.mult),
                    reads=[gate_sb[:, i, h * 256:(h + 1) * 256], gst[:, 48 + col:49 + col]],
                    writes=[sc_sb[:, 0:256]])
                S.op("vector", lambda E, i=i, h=h, col=col: E.scalar_tensor_tensor(
                    gg_sb[:, i, h * 256:(h + 1) * 256],
                    ao_sb[:, i, h * 256:(h + 1) * 256],
                    gst[:, 32 + col:33 + col], sc_sb[:, 0:256],
                    ALU.add, ALU.mult),
                    reads=[ao_sb[:, i, h * 256:(h + 1) * 256], gst[:, 32 + col:33 + col],
                           sc_sb[:, 0:256]],
                    writes=[gg_sb[:, i, h * 256:(h + 1) * 256]])

    def ggt_proj_phase(c, post_wave=None, mid_w1=None):
        for i in range(MT):
            for vg in range(2):
                srcs = [gg_sb[:, i, (vg * 8 + v) * 128:(vg * 8 + v + 1) * 128]
                        for v in range(8)]
                dv_ = ggT_sb[:, vg * 8:(vg + 1) * 8, i * 128:(i + 1) * 128]
                trans_group(srcs, dv_)
        xb = x_sb[c % 3]
        for wave in range(2):
            for vtp in range(8):
                if wave == 1 and vtp == 2 and mid_w1 is not None:
                    mid_w1()
                load_wp(vtp)
                for sub in range(2):
                    vt = 2 * vtp + sub
                    for fo_ in range(4):
                        f = wave * 4 + fo_
                        P = [PA, PB][fo_ // 2]
                        fo = (fo_ % 2) * 512
                        S.op("tensor", lambda E, vtp=vtp, sub=sub, vt=vt, f=f, P=P, fo=fo: E.matmul(
                            P[:, fo:fo + C], wp_sb[vtp % 4][:, sub, f * 128:(f + 1) * 128],
                            ggT_sb[:, vt, :], start=(vt == 0), stop=(vt == 15)),
                            reads=[wp_sb[vtp % 4][:, sub, f * 128:(f + 1) * 128],
                                   ggT_sb[:, vt, :]],
                            writes=[P[:, fo:fo + C]])
            for fo_ in range(4):
                f = wave * 4 + fo_
                P = [PA, PB][fo_ // 2]
                fo = (fo_ % 2) * 512
                sc = pe_sc[f % 2]
                S.op("scalar", lambda E, P=P, fo=fo, sc=sc: E.activation(
                    sc[:], P[:, fo:fo + C], AF.Copy),
                    reads=[P[:, fo:fo + C]], writes=[sc[:]])
                S.op("gpsimd", lambda E, f=f, xb=xb, sc=sc: E.tensor_tensor(
                    xp_sb[:, f, :], sc[:], xb[:, f, :], ALU.add),
                    reads=[sc[:], xb[:, f, :]], writes=[xp_sb[:, f, :]])
            if post_wave is not None:
                post_wave(wave)
        xpv = d_xp[:, c]
        S.dma("scalar", lambda E, o=xpv: E.dma_start(out=o, in_=xp_sb[:]),
              reads=[xp_sb[:]], writes=[xpv])

    def b1_phase_a(c):
        ln_stats(rows_b, combined=True)

    def b1_bcast(c):
        S.op("tensor", lambda E: E.matmul(PC[:], ones_f[:], rows_b[:, 0:512],
                                          start=True, stop=True),
             reads=[ones_f[:], rows_b[:, 0:512]], writes=[PC[:]])
        S.op("vector", lambda E: E.tensor_copy(bc_sb[:], PC[:]),
             reads=[PC[:]], writes=[bc_sb[:]])

    def b1_rest(c):
        rbc = bc_sb[:, 0:256].unsqueeze(1).broadcast_to([128, KF, C])
        mbc = bc_sb[:, 256:512].unsqueeze(1).broadcast_to([128, KF, C])
        S.op("vector", lambda E, rbc=rbc: E.tensor_tensor(
            tmp_f[:], xp_sb[:], rbc, ALU.mult),
            reads=[xp_sb[:], bc_sb[:, 0:256]], writes=[tmp_f[:]])
        S.op("vector", lambda E, mbc=mbc: E.tensor_tensor(
            out_sb[:], tmp_f[:], mbc, ALU.subtract),
            reads=[tmp_f[:], bc_sb[:, 256:512]], writes=[out_sb[:]])
        for k in range(KF):
            S.op("vector", lambda E, k=k: E.tensor_scalar(
                out_sb[:, k, :], out_sb[:, k, :], lng[:, k:k + 1], None, ALU.mult),
                reads=[out_sb[:, k, :], lng[:, k:k + 1]], writes=[out_sb[:, k, :]])
        S.op("vector", lambda E: E.tensor_reduce(gst[:, 0:8], out_sb[:], AX.X, ALU.add),
             reads=[out_sb[:]], writes=[gst[:, 0:8]])
        S.op("vector", lambda E: E.tensor_add(ysent[:], ysent[:], gst[:, 0:8]),
             reads=[ysent[:], gst[:, 0:8]], writes=[ysent[:]])
        zv = d_zt[:, c]
        S.dma("gpsimd", lambda E, o=zv: E.dma_start(out=o, in_=out_sb[:]),
              reads=[out_sb[:]], writes=[zv])

    if KLVL >= 1:
        NRUN = min(KNCH, NCH)
        preload(0)
        preload(1)
        front_a(0)
        front_b(0)
        qk_pre(0)
        qk_phase(0)
        vgate_phase(0)
        qkt_phase(0)
        if NRUN > 1:
            front_a(1)
            front_b(1)
        for c in range(NRUN):
            if c + 1 < NRUN:
                qk_pre(c + 1)
            preload(c + 2)
            retention_phase(c)
            if c + 1 < NRUN:
                qk_phase(c + 1)
            gn_phase(c)
            if c + 1 < NRUN:
                vgate_phase(c + 1)
                qkt_phase(c + 1)
            if KLVL >= 5 and c > 0:
                b1_rest(c - 1)
            if c + 2 < NRUN:
                front_a(c + 2)
            ggt_proj_phase(
                c, post_wave=(lambda w: ln_cast_half(xp_sb, w)) if KLVL >= 5 else None)
            if KLVL >= 5:
                b1_phase_a(c)
            if c + 2 < NRUN:
                front_b(c + 2)
            if KLVL >= 5:
                b1_bcast(c)
        if KLVL >= 5:
            b1_rest(NRUN - 1)

    if KLVL >= 7:
        for cc in range(2):
            zvp = d_zt[:, cc]
            S.dma("sync", lambda E, o=zbuf[cc], i=zvp: E.dma_start(out=o, in_=i),
                  reads=[zvp], writes=[zbuf[cc]])
            xvp = d_xp[:, cc]
            S.dma("gpsimd", lambda E, o=x_sb[cc], i=xvp: E.dma_start(
                out=o[:], in_=i), reads=[xvp], writes=[x_sb[cc][:]])
    if KLVL >= 6:
        S.op("vector", lambda E: E.tensor_scalar(zs_sb[:], ysent[:], 1.0 / T, None, ALU.mult),
             reads=[ysent[:]], writes=[zs_sb[:]])
        S.op("vector", lambda E: E.tensor_add(zs_sb[:], zs_sb[:], lnb[:]),
             reads=[zs_sb[:], lnb[:]], writes=[zs_sb[:]])
        S.op("vector", lambda E: E.tensor_sub(zb_sb[:], zs_sb[:], bnm[:]),
             reads=[zs_sb[:], bnm[:]], writes=[zb_sb[:]])
        S.op("scalar", lambda E: E.activation(gst[:, 0:8], bnv[:], AF.Ln, bias=eps_c[:]),
             reads=[bnv[:], eps_c[:]], writes=[gst[:, 0:8]])
        S.op("scalar", lambda E: E.activation(gst[:, 8:16], gst[:, 0:8], AF.Exp, scale=-0.5),
             reads=[gst[:, 0:8]], writes=[gst[:, 8:16]])
        S.op("vector", lambda E: E.tensor_tensor(zb_sb[:], zb_sb[:], gst[:, 8:16], ALU.mult),
             reads=[zb_sb[:], gst[:, 8:16]], writes=[zb_sb[:]])
        S.op("vector", lambda E: E.tensor_tensor(zb_sb[:], zb_sb[:], bng[:], ALU.mult),
             reads=[zb_sb[:], bng[:]], writes=[zb_sb[:]])
        S.op("vector", lambda E: E.tensor_add(zb_sb[:], zb_sb[:], bnb[:]),
             reads=[zb_sb[:], bnb[:]], writes=[zb_sb[:]])
        for k in range(KF):
            S.op("tensor", lambda E, k=k: E.matmul(
                PC[0:1, 0:8], zb_sb[:, k:k + 1], wr_sb[:, k * 8:(k + 1) * 8],
                start=(k == 0), stop=(k == KF - 1)),
                reads=[zb_sb[:, k:k + 1], wr_sb[:, k * 8:(k + 1) * 8]],
                writes=[PC[0:1, 0:8]])
        S.op("vector", lambda E: E.tensor_tensor(sm_sb[:, 0:8], PC[0:1, 0:8], br_sb[:], ALU.add),
             reads=[PC[0:1, 0:8], br_sb[:]], writes=[sm_sb[:, 0:8]])
        S.op("vector", lambda E: E.reduce_max(sm_sb[:, 24:25], sm_sb[:, 0:8], AX.X),
             reads=[sm_sb[:, 0:8]], writes=[sm_sb[:, 24:25]])
        S.op("vector", lambda E: E.tensor_scalar(sm_sb[:, 25:26], sm_sb[:, 24:25],
                                                 -1.0, None, ALU.mult),
             reads=[sm_sb[:, 24:25]], writes=[sm_sb[:, 25:26]])
        S.op("scalar", lambda E: E.activation(sm_sb[:, 8:16], sm_sb[:, 0:8], AF.Exp,
                                              bias=sm_sb[:, 25:26]),
             reads=[sm_sb[:, 0:8], sm_sb[:, 25:26]], writes=[sm_sb[:, 8:16]])
        S.op("vector", lambda E: E.reduce_sum(sm_sb[:, 26:27], sm_sb[:, 8:16], AX.X),
             reads=[sm_sb[:, 8:16]], writes=[sm_sb[:, 26:27]])
        S.op("vector", lambda E: E.reciprocal(sm_sb[:, 27:28], sm_sb[:, 26:27]),
             reads=[sm_sb[:, 26:27]], writes=[sm_sb[:, 27:28]])
        S.op("vector", lambda E: E.tensor_scalar(sm_sb[:, 16:24], sm_sb[:, 8:16],
                                                 1.0, None, ALU.is_ge),
             reads=[sm_sb[:, 8:16]], writes=[sm_sb[:, 16:24]])
        S.op("tensor", lambda E: E.transpose(PD[0:8, 0:1], sm_sb[:, 16:24], ones_f[0:1, 0:1]),
             reads=[sm_sb[:, 16:24], ones_f[0:1, 0:1]], writes=[PD[0:8, 0:1]])
        S.op("tensor", lambda E: E.matmul(PC[:, 0:8], ones_f[:], sm_sb[:, 16:24],
                                          start=True, stop=True),
             reads=[ones_f[:], sm_sb[:, 16:24]], writes=[PC[:, 0:8]])
        S.op("tensor", lambda E: E.matmul(PC[:, 8:9], ones_f[:], sm_sb[:, 27:28],
                                          start=True, stop=True),
             reads=[ones_f[:], sm_sb[:, 27:28]], writes=[PC[:, 8:9]])
        S.op("vector", lambda E: E.tensor_copy(ohT[:], PD[0:8, 0:1]),
             reads=[PD[0:8, 0:1]], writes=[ohT[:]])
        S.op("vector", lambda E: E.tensor_copy(ohbc[:], PC[:, 0:8]),
             reads=[PC[:, 0:8]], writes=[ohbc[:]])
        S.op("vector", lambda E: E.tensor_copy(gatebc[:], PC[:, 8:9]),
             reads=[PC[:, 8:9]], writes=[gatebc[:]])
        for e in range(8):
            S.dma("gpsimd", lambda E, e=e, o=we_sb[e % 2]: E.dma_start(
                out=o, in_=d_wd[:, e * 512:(e + 1) * 512]),
                reads=[d_wd[:, e * 512:(e + 1) * 512]], writes=[we_sb[e % 2]])
            S.dma("gpsimd", lambda E, e=e, o=wue_sb[e % 2]: E.dma_start(
                out=o, in_=d_wu[:, e * 1024:(e + 1) * 1024]),
                reads=[d_wu[:, e * 1024:(e + 1) * 1024]], writes=[wue_sb[e % 2]])
            if e == 0:
                S.op("vector", lambda E: E.tensor_scalar(
                    acc[0], we_sb[0], ohbc[:, 0:1], None, ALU.mult),
                    reads=[we_sb[0], ohbc[:, 0:1]], writes=[acc[0]])
                S.op("vector", lambda E: E.tensor_scalar(
                    uacc[0], wue_sb[0], ohbc[0:64, 0:1], None, ALU.mult),
                    reads=[wue_sb[0], ohbc[0:64, 0:1]], writes=[uacc[0]])
                S.op("vector", lambda E: E.tensor_scalar(
                    bug[0][:], bu_sb[:, 0:8], ohbc[:, 0:1], None, ALU.mult),
                    reads=[bu_sb[:, 0:8], ohbc[:, 0:1]], writes=[bug[0][:]])
            else:
                S.op("vector", lambda E, e=e: E.scalar_tensor_tensor(
                    acc[e % 2], we_sb[e % 2], ohbc[:, e:e + 1], acc[(e - 1) % 2],
                    ALU.mult, ALU.add),
                    reads=[we_sb[e % 2], ohbc[:, e:e + 1], acc[(e - 1) % 2]],
                    writes=[acc[e % 2]])
                S.op("vector", lambda E, e=e: E.scalar_tensor_tensor(
                    uacc[e % 2], wue_sb[e % 2], ohbc[0:64, e:e + 1], uacc[(e - 1) % 2],
                    ALU.mult, ALU.add),
                    reads=[wue_sb[e % 2], ohbc[0:64, e:e + 1], uacc[(e - 1) % 2]],
                    writes=[uacc[e % 2]])
                S.op("vector", lambda E, e=e: E.scalar_tensor_tensor(
                    bug[e % 2][:], bu_sb[:, e * 8:(e + 1) * 8], ohbc[:, e:e + 1],
                    bug[(e - 1) % 2][:], ALU.mult, ALU.add),
                    reads=[bu_sb[:, e * 8:(e + 1) * 8], ohbc[:, e:e + 1], bug[(e - 1) % 2][:]],
                    writes=[bug[e % 2][:]])
        wd_eff, wu_eff, bu_eff = acc[1], uacc[1], bug[1]
        S.op("vector", lambda E: E.tensor_copy(wd_bf, wd_eff),
             reads=[wd_eff], writes=[wd_bf])
        S.op("vector", lambda E: E.tensor_scalar(uacc[0], wu_eff, gatebc[0:64, 0:1],
                                                 None, ALU.mult),
             reads=[wu_eff, gatebc[0:64, 0:1]], writes=[uacc[0]])
        S.op("vector", lambda E: E.tensor_copy(wu_bf, uacc[0]),
             reads=[uacc[0]], writes=[wu_bf])
        S.op("vector", lambda E: E.tensor_scalar(bug[0][:], bu_eff[:], gatebc[:, 0:1],
                                                 None, ALU.mult),
             reads=[bu_eff[:], gatebc[:, 0:1]], writes=[bug[0][:]])
        bu_g = bug[0]
        S.op("vector", lambda E: E.tensor_scalar(sc_sb[0:8, 0:64], bd_sb[:], ohT[:],
                                                 None, ALU.mult),
             reads=[bd_sb[:], ohT[:]], writes=[sc_sb[0:8, 0:64]])
        for k in range(KF):
            S.op("tensor", lambda E, k=k: E.matmul(
                PC[0:1, 64:128], lnb[:, k:k + 1], wd_eff[:, k * 64:(k + 1) * 64],
                start=(k == 0), stop=False),
                reads=[lnb[:, k:k + 1], wd_eff[:, k * 64:(k + 1) * 64]],
                writes=[PC[0:1, 64:128]])
        S.op("tensor", lambda E: E.matmul(
            PC[0:1, 64:128], ones_c[0:8, :], sc_sb[0:8, 0:64], start=False, stop=True),
            reads=[ones_c[0:8, :], sc_sb[0:8, 0:64]], writes=[PC[0:1, 64:128]])
        S.op("vector", lambda E: E.tensor_copy(sc_sb[0:1, 0:64], PC[0:1, 64:128]),
             reads=[PC[0:1, 64:128]], writes=[sc_sb[0:1, 0:64]])
        S.op("tensor", lambda E: E.transpose(PD[0:64, 0:1], sc_sb[0:1, 0:64],
                                             ones_f[0:1, 0:1]),
             reads=[sc_sb[0:1, 0:64], ones_f[0:1, 0:1]], writes=[PD[0:64, 0:1]])
        S.op("vector", lambda E: E.tensor_copy(biash[:], PD[0:64, 0:1]),
             reads=[PD[0:64, 0:1]], writes=[biash[:]])

    if KLVL >= 7:
        def b2_h(c):
            zb = zbuf[c % 2]
            hb = h_bf2[c % 2]
            for k in range(KF):
                S.op("tensor", lambda E, k=k, zb=zb: E.matmul(
                    PC[0:64, 0:256], wd_bf[:, k * 64:(k + 1) * 64], zb[:, k, :],
                    start=(k == 0), stop=(k == KF - 1)),
                    reads=[wd_bf[:, k * 64:(k + 1) * 64], zb[:, k, :]],
                    writes=[PC[0:64, 0:256]])
            S.op("scalar", lambda E, hb=hb: E.activation(hb[:], PC[0:64, 0:256], AF.Relu,
                                                         bias=biash[:]),
                 reads=[PC[0:64, 0:256], biash[:]], writes=[hb[:]])
            if c + 2 < NCH:
                zv2 = d_zt[:, c + 2]
                S.dma("sync", lambda E, o=zbuf[(c + 2) % 2], i=zv2: E.dma_start(
                    out=o, in_=i), reads=[zv2], writes=[zbuf[(c + 2) % 2]])

        def b2_y(c):
            xb = x_sb[c % 3]
            hb = h_bf2[c % 2]
            for f in range(KF):
                P = PA if f < 4 else PB
                fo = (f % 4) * 256
                S.op("tensor", lambda E, f=f, P=P, fo=fo, hb=hb: E.matmul(
                    P[:, fo:fo + C], wu_bf[:, f * 128:(f + 1) * 128], hb[:],
                    start=True, stop=True),
                    reads=[wu_bf[:, f * 128:(f + 1) * 128], hb[:]],
                    writes=[P[:, fo:fo + C]])
            for f in range(KF):
                P = PA if f < 4 else PB
                fo = (f % 4) * 256
                S.op("vector", lambda E, f=f, P=P, fo=fo, xb=xb, bu_g=bu_g: E.scalar_tensor_tensor(
                    out_sb[:, f, :], P[:, fo:fo + C], bu_g[:, f:f + 1], xb[:, f, :],
                    ALU.add, ALU.add),
                    reads=[P[:, fo:fo + C], bu_g[:, f:f + 1], xb[:, f, :]],
                    writes=[out_sb[:, f, :]])
            ov = d_out[:, c]
            S.dma("scalar", lambda E, o=ov: E.dma_start(out=o, in_=out_sb[:]),
                  reads=[out_sb[:]], writes=[ov])
            if c + 2 < NCH:
                xv2 = d_xp[:, c + 2]
                S.dma("gpsimd", lambda E, o=x_sb[(c + 2) % 3], i=xv2: E.dma_start(
                    out=o[:], in_=i), reads=[xv2], writes=[x_sb[(c + 2) % 3][:]])

        for c in range(NCH):
            b2_h(c)
            if c > 0:
                b2_y(c - 1)
        b2_y(NCH - 1)

    return S.emit(nc)



def _host_constants():
    bf = ml_dtypes.bfloat16
    QK_DIM, SCALE_BASE = 1024, 512.0
    dim_half = QK_DIM // 2
    base_scale = ((np.arange(0, QK_DIM, 2) + 0.4 * QK_DIM) / (1.4 * QK_DIM)).astype(np.float64)
    pos = np.arange(T, dtype=np.float64)
    scale = (base_scale[None, :] ** (pos[:, None] / SCALE_BASE))
    inv_freq = 1.0 / 10000.0 ** (np.arange(dim_half) / dim_half)
    sinus = pos[:, None] * inv_freq[None, :]
    sin, cos = np.sin(sinus), np.cos(sinus)
    dkinv = (QK_DIM // HH) ** -0.5
    tabs = np.concatenate([cos * scale * dkinv, sin * scale * dkinv,
                           cos / scale, sin / scale], axis=1).astype(np.float32).astype(bf)
    ident = np.eye(128, dtype=np.float32).astype(bf)
    tri = np.triu(np.ones((128, 128), np.float32)).astype(bf)
    iota = np.arange(C, dtype=np.float32).reshape(1, C)
    return tabs, ident, tri, iota


_QKPERM = None


def _qk_perm():
    global _QKPERM
    if _QKPERM is None:
        perm = []
        for h in range(HH):
            perm += [128 * h + 2 * r for r in range(64)]
            perm += [128 * h + 2 * r + 1 for r in range(64)]
        _QKPERM = np.asarray(perm)
    return _QKPERM


def _col8(v):
    return np.ascontiguousarray(np.asarray(v, np.float32).reshape(8, 128).T)


def kernel(L0_attn_hidden_states, L0_attn_W_qkv, L0_attn_W_g, L0_attn_W_proj,
           L0_attn_decay, L0_ffn_ln_g, L0_ffn_ln_b, L0_ffn_bn_g, L0_ffn_bn_b,
           L0_ffn_bn_mean, L0_ffn_bn_var, L0_ffn_Wr, L0_ffn_br, L0_ffn_W_down,
           L0_ffn_b_down, L0_ffn_W_up, L0_ffn_b_up):
    x = np.asarray(L0_attn_hidden_states, np.float32)
    Wqkv = np.asarray(L0_attn_W_qkv, np.float32)
    Wg = np.asarray(L0_attn_W_g, np.float32)
    Wproj = np.ascontiguousarray(np.asarray(L0_attn_W_proj, np.float32))
    perm = _qk_perm()
    Wq = Wqkv[:, 0:1024][:, perm]
    Wk = Wqkv[:, 1024:2048][:, perm]
    Wv = Wqkv[:, 2048:4096]
    wqkvg = np.ascontiguousarray(np.concatenate([Wq, Wk, Wv, Wg], axis=1))

    tabs, ident, tri, iota = _host_constants()
    tabs = np.ascontiguousarray(
        tabs.reshape(NCH, MT, 128, 2048).transpose(2, 0, 1, 3))
    decay = np.asarray(L0_attn_decay, np.float32).reshape(1, 8)
    lng = _col8(L0_ffn_ln_g)
    lnb = _col8(L0_ffn_ln_b)
    bnm = _col8(L0_ffn_bn_mean)
    bnv = _col8(L0_ffn_bn_var)
    bng = _col8(L0_ffn_bn_g)
    bnb = _col8(L0_ffn_bn_b)
    Wr = np.asarray(L0_ffn_Wr, np.float32)
    wr = np.ascontiguousarray(Wr.reshape(8, 128, 8).transpose(1, 0, 2).reshape(128, 64))
    br = np.asarray(L0_ffn_br, np.float32).reshape(1, 8)
    Wd = np.asarray(L0_ffn_W_down, np.float32)
    wd = np.ascontiguousarray(Wd.reshape(8, 8, 128, 64).transpose(2, 0, 1, 3).reshape(128, 4096))
    bd = np.ascontiguousarray(np.asarray(L0_ffn_b_down, np.float32))
    Wu = np.asarray(L0_ffn_W_up, np.float32)
    wu = np.ascontiguousarray(Wu.transpose(1, 0, 2).reshape(64, 8192))
    bu = np.asarray(L0_ffn_b_up, np.float32)
    bu_col = np.ascontiguousarray(bu.T.reshape(8, 128, 8).transpose(1, 2, 0).reshape(128, 64))

    nc = build_nc()
    in_maps = []
    for b in range(8):
        xb_t = x[b].T.reshape(KF, 128, NCH, C).transpose(1, 2, 0, 3)
        in_maps.append({
            "xt": np.ascontiguousarray(xb_t),
            "wqkvg": wqkvg, "wproj": Wproj, "tabs": tabs, "ident_d": ident,
            "tri_d": tri, "iota_d": iota, "decay_d": decay, "lng_d": lng,
            "lnb_d": lnb, "bnm_d": bnm, "bnv_d": bnv, "bng_d": bng, "bnb_d": bnb,
            "wr": wr, "br_d": br, "wd_d": wd, "bd_d": bd, "wu_d": wu, "bu_d": bu_col,
        })
    res = run_bass_kernel_spmd(nc, in_maps, core_ids=list(range(8)))
    out = np.empty((8, T, D), np.float32)
    for b in range(8):
        a = np.asarray(res.results[b]["outT"], np.float32)
        out[b] = a.transpose(2, 0, 1, 3).reshape(D, T).T
    return out
